# version 44
# baseline (speedup 1.0000x reference)
import numpy as np
import ml_dtypes

import concourse.bass as bass
import concourse.mybir as mybir
from concourse.bass_utils import run_bass_kernel_spmd

F32 = mybir.dt.float32
BF16 = mybir.dt.bfloat16
AF = mybir.ActivationFunctionType
ALU = mybir.AluOpType
AX = mybir.AxisListType
BF = ml_dtypes.bfloat16

D = 2048
SEQ = 2048
H = 16
E = 128
T = 1024
NS = 8
FF = 8192
DT = D // 128
EXT = [2, 4, 6, 8, 10, 12, 14, 16]
NQ = 4
HPQ = H // NQ
SCALE = float(1.0 / np.sqrt(128.0))
NFC = 16
FCW = FF // NFC

BLOCKS_EVEN = [0, 3, 4, 7, 8, 11, 12, 15]
BLOCKS_ODD = [1, 2, 5, 6, 9, 10, 13, 14]
NSB = 5

ENGINES = ("pe", "act", "dve", "pool", "sp")
_BLOCK_ATTR = {"pe": "tensor", "act": "scalar", "dve": "vector",
               "pool": "gpsimd", "sp": "sync"}


class Plan:

    def __init__(self, nc, serial=False):
        self.nc = nc
        self.serial = serial
        self.items = {e: [] for e in ENGINES}
        self.count = {}
        self.hwm = {e: {} for e in ENGINES}
        self.sems = {}
        self.msem = {e: "m_" + e for e in ENGINES}

    def sem(self, name):
        if name not in self.sems:
            self.sems[name] = self.nc.alloc_semaphore(name=name)
        return self.sems[name]

    def op(self, eng, fn, waits=(), inc=None, inc_amt=1, waits_hard=()):
        import sys
        f = sys._getframe(1)
        if f.f_code.co_name in ("mop", "dma"):
            f = f.f_back
        self.tags = getattr(self, "tags", {e: [] for e in ENGINES})
        self.tags[eng].append(f.f_lineno)
        if self.serial and fn is not None:
            waits = list(waits) + [(sn, v) for sn, v in self.count.items()]
        ws = []
        own = self.msem[eng]
        for sn, v in waits_hard:
            if v is None or v <= 0:
                continue
            if self.hwm[eng].get(sn, 0) >= v:
                continue
            self.hwm[eng][sn] = v
            ws.append((sn, v))
        for sn, v in waits:
            if v is None or v <= 0:
                continue
            if self.hwm[eng].get(sn, 0) >= v:
                continue
            self.hwm[eng][sn] = v
            ws.append((sn, v))
        ev = None
        if inc is not None:
            self.count[inc] = self.count.get(inc, 0) + inc_amt
            ev = (inc, self.count[inc])
        self.items[eng].append((ws, fn, inc, inc_amt))
        return ev

    def mop(self, eng, fn, waits=(), waits_hard=()):
        return self.op(eng, fn, waits=waits, inc=self.msem[eng], waits_hard=waits_hard)

    def dma(self, queue, out_ap, in_ap, sem_name, waits=()):
        return self.op(queue, lambda e, o=out_ap, i=in_ap: e.dma_start(o, i),
                       waits=waits, inc=sem_name, inc_amt=16)

    def barrier_wait(self, eng, waits):
        self.op(eng, None, waits=waits)

    def emit(self, block):
        for eng in ENGINES:
            items = self.items[eng]

            def body(e, items=items):
                for ws, fn, inc, amt in items:
                    for sn, v in ws:
                        e.wait_ge(self.sem(sn), v)
                    if fn is None:
                        continue
                    ins = fn(e)
                    if inc is not None:
                        ins.then_inc(self.sem(inc), amt)
            getattr(block, _BLOCK_ATTR[eng])(body)

    def stats(self):
        return {e: len(v) for e, v in self.items.items()}


def emax(*evs):
    best = {}
    for ev in evs:
        if ev is None:
            continue
        if isinstance(ev, list):
            for sn, v in ev:
                if v is not None:
                    best[sn] = max(best.get(sn, 0), v)
            continue
        sn, v = ev
        if v is None:
            continue
        best[sn] = max(best.get(sn, 0), v)
    return list(best.items())


def build_program(debug=False, stage="full", serial=False):
    if debug:
        stage = "full"
    nc = bass.Bass("TRN2", target_bir_lowering=False)

    xb_d = nc.dram_tensor("xb", [SEQ, D], F32, kind="ExternalInput")
    xq_d = nc.dram_tensor("xq", [T, D], F32, kind="ExternalInput")
    wq_d = nc.dram_tensor("wq", [H * 128, DT * 128], F32, kind="ExternalInput")
    wk_d = nc.dram_tensor("wk", [H * 128, DT * 128], F32, kind="ExternalInput")
    wv_d = nc.dram_tensor("wv", [NQ * 128, DT * 512], F32, kind="ExternalInput")
    wo_d = nc.dram_tensor("wo", [4 * 128, DT * 512], F32, kind="ExternalInput")
    w1t_d = nc.dram_tensor("w1t", [D, FF], F32, kind="ExternalInput")
    w2t_d = nc.dram_tensor("w2t", [FF, D], F32, kind="ExternalInput")
    b1r_d = nc.dram_tensor("b1r", [128, 64], F32, kind="ExternalInput")
    b2_d = nc.dram_tensor("b2", [1, D], F32, kind="ExternalInput")
    masks_d = nc.dram_tensor("masks", [128, NS * 256], BF16, kind="ExternalInput")
    out_d = nc.dram_tensor("out", [T, D], F32, kind="ExternalOutput")

    dbg = stage != "full"
    full_like = stage in ("full", "F")
    ikind = "ExternalOutput" if dbg else "Internal"
    qT_dram = nc.dram_tensor("qT_dram", [D, T], BF16, kind=ikind)
    zT_dram = nc.dram_tensor("zT_dram", [NS * 128, DT * 128], BF16, kind=ikind)
    scr_dram = nc.dram_tensor("scr_dram", [2, 128], F32, kind="Internal")
    if dbg:
        dbg_lnxT = nc.dram_tensor("dbg_lnxT", [128, DT * SEQ], BF16, kind="ExternalOutput")
        dbg_lnqT = nc.dram_tensor("dbg_lnqT", [128, DT * T], BF16, kind="ExternalOutput")
        dbg_kT = nc.dram_tensor("dbg_kT", [128, HPQ * SEQ], BF16, kind="ExternalOutput")
        dbg_v = nc.dram_tensor("dbg_v", [128, 16 * 512], BF16, kind="ExternalOutput")
        dbg_x1 = nc.dram_tensor("dbg_x1", [T, D], F32, kind="ExternalOutput")
        dbg_ln2T = nc.dram_tensor("dbg_ln2T", [128, DT * T], BF16, kind="ExternalOutput")
        dbg_t = nc.dram_tensor("dbg_t", [128, 2], F32, kind="ExternalOutput")
        dbg_parts = nc.dram_tensor("dbg_parts", [128, 128], F32, kind="ExternalOutput")
        dbg_w1q = nc.dram_tensor("dbg_w1q", [128, DT * FCW], BF16, kind="ExternalOutput")
        dbg_w2q = nc.dram_tensor("dbg_w2q", [128, 4 * D], BF16, kind="ExternalOutput")
        dbg_gT = nc.dram_tensor("dbg_gT", [128, 4 * T], BF16, kind="ExternalOutput")
        dbg_S = nc.dram_tensor("dbg_S", [128, SEQ], F32, kind="ExternalOutput")
        dbg_pat = nc.dram_tensor("dbg_pat", [128, SEQ], BF16, kind="ExternalOutput")
        dbg_patT = nc.dram_tensor("dbg_patT", [128, SEQ], BF16, kind="ExternalOutput")
        dbg_nm = nc.dram_tensor("dbg_nm", [128, 4], F32, kind="ExternalOutput")
        dbg_bn = nc.dram_tensor("dbg_bn", [128, 2], F32, kind="ExternalOutput")
        dbg_lnbf = nc.dram_tensor("dbg_lnbf", [128, D], BF16, kind="ExternalOutput")

    AL = nc.alloc_sbuf_tensor
    base0 = nc.sbuf_base
    id128 = AL("id128", [128, 128], BF16)
    ones_bf = AL("ones_bf", [1, 128], BF16)
    ones_f = AL("ones_f", [1, 128], F32)
    epsc = AL("epsc", [128, 1], F32)
    b1sb = AL("b1sb", [128, 64], F32)
    b2row = AL("b2row", [1, D], BF16)
    t1b = AL("t1b", [128, 1], F32)
    t2b = AL("t2b", [128, 1], F32)
    negt2b = AL("negt2b", [128, 1], F32)
    onescol = AL("onescol", [128, 1], F32)
    part1 = AL("part1", [128, 256], F32)
    part2 = AL("part2", [128, 256], F32)
    scal2 = AL("scal2", [1, 2], F32)
    bn2 = [AL(f"bn2{i}", [128, 2], F32) for i in range(2)]
    ssq4 = [AL(f"ssq4{i}", [128, 4], F32) for i in range(2)]
    sum4 = [AL(f"sum4{i}", [128, 4], F32) for i in range(2)]
    msq = [AL(f"msq{i}", [128, 1], F32) for i in range(2)]
    tmp1 = [AL(f"tmp1{i}", [128, 1], F32) for i in range(2)]
    rstd = [AL(f"rstd{i}", [128, 1], F32) for i in range(2)]
    sume4 = [AL(f"sume4{i}", [128, 4], F32) for i in range(NSB)]
    negt1b = AL("negt1b", [128, 1], F32)
    sume = [AL(f"sume{i}", [128, 1], F32) for i in range(NSB)]
    rs = [AL(f"rs{i}", [128, 1], F32) for i in range(NSB)]
    qov_base = nc.sbuf_base
    masksb = AL("masksb", [128, NS * 256], BF16)
    xstage = [AL(f"xstage{i}", [128, D], F32) for i in range(2)]
    lnbf_off = nc.sbuf_base
    lnbf = AL("lnbf", [128, D], BF16)
    qov_end = nc.sbuf_base

    M0 = nc.sbuf_base

    sub0 = nc.sbuf_base
    lnq2T = AL("lnq2T", [128, DT * T], BF16)
    sub_end = nc.sbuf_base
    nc.sbuf_base = sub0
    Sbuf = [AL(f"S{i}", [128, SEQ], BF16) for i in range(NSB)]
    diagm = [AL(f"diagm{i}", [128, 128], BF16) for i in range(NSB)]
    patT = [AL(f"patT{i}", [128, SEQ], BF16) for i in range(2)]
    zTq = AL("zTq", [128, HPQ * T], BF16)
    nc.sbuf_base = max(nc.sbuf_base, sub_end)

    qstage = [AL(f"qstage{i}", [128, 512], BF16) for i in range(2)]
    qTq_off = nc.sbuf_base
    qTq = AL("qTq", [128, HPQ * T], BF16)
    sv2 = nc.sbuf_base
    nc.sbuf_base = qTq_off
    xstage.append(AL("xstage2", [128, D], F32))
    assert nc.sbuf_base <= sv2
    nc.sbuf_base = sv2
    lnxT = AL("lnxT", [128, DT * SEQ], BF16)
    kTb = AL("kTb", [128, HPQ * SEQ], BF16)
    vqb = AL("vqb", [128, 16 * 512], BF16)
    wkbuf = [AL(f"wkbuf{i}", [128, DT * 128], BF16) for i in range(3)]
    wvq = AL("wvq", [128, DT * 512], BF16)
    att_end = nc.sbuf_base
    assert att_end <= nc.sbuf_top, f"ATT overflow: {att_end} > {nc.sbuf_top}"

    nc.sbuf_base = M0
    x1 = AL("x1", [128, NS * D], F32)
    ln2T = AL("ln2T", [128, DT * T], BF16)
    M1 = nc.sbuf_base
    wobuf = [AL(f"wobuf{i}", [128, DT * 512], BF16) for i in range(2)]
    zTop = [AL(f"zTop{i}", [128, DT * 128], BF16) for i in range(NS)]
    op_end = nc.sbuf_base
    assert op_end <= nc.sbuf_top, "OP overflow"
    nc.sbuf_base = M1
    w1q = [AL(f"w1q{i}", [128, DT * FCW], BF16) for i in range(2)]
    w2q = [AL(f"w2q{i}", [128, 4 * D], BF16) for i in range(2)]
    gTb = AL("gTb", [128, 4 * T], BF16)
    ffn_end = nc.sbuf_base
    assert ffn_end <= nc.sbuf_top, f"FFN overflow {ffn_end} > {nc.sbuf_top}"
    nc.sbuf_base = qov_base
    qst1 = [AL(f"qst1{i}", [128, 1024], F32) for i in range(2)]
    qab1 = [AL(f"qab1{i}", [128, 1024], BF16) for i in range(2)]
    qst2 = [AL(f"qst2{i}", [128, 512], F32) for i in range(2)]
    qsg2 = [AL(f"qsg2{i}", [128, 512], BF16) for i in range(2)]
    assert nc.sbuf_base <= qov_end, f"quant overlay overflow {nc.sbuf_base} > {qov_end}"
    nc.sbuf_base = max(att_end, op_end, ffn_end)

    from contextlib import ExitStack
    stack = ExitStack()
    ps_sc = [stack.enter_context(nc.psum_tensor(f"ps_sc{i}", [128, 512], F32)) for i in range(2)]
    ps_z = [stack.enter_context(nc.psum_tensor(f"ps_z{i}", [128, 512], F32)) for i in range(2)]
    ps_kv = [stack.enter_context(nc.psum_tensor(f"ps_kv{i}", [128, 512], F32)) for i in range(2)]
    psT = [stack.enter_context(nc.psum_tensor(f"psT{i}", [128, 1024], BF16)) for i in range(2)]

    P = Plan(nc, serial=serial)

    def finish(waits):
        P.barrier_wait("sp", waits)
        with nc.Block() as block:
            P.emit(block)
        stack.close()
        return nc, P

    ev_ms = P.mop("pool", lambda e: e.memset(id128[:, :], 0.0))
    ev_id = P.mop("pool", lambda e: e.affine_select(
        out=id128[:, :], in_=id128[:, :], compare_op=ALU.not_equal,
        fill=1.0, base=0, pattern=[[-1, 128]], channel_multiplier=1),
        waits_hard=emax(ev_ms))
    ev_ones = P.mop("pool", lambda e: e.memset(ones_bf[:, :], 1.0))
    ev_onesf = P.mop("pool", lambda e: e.memset(ones_f[:, :], 1.0))
    ev_onescol = P.mop("pool", lambda e: e.memset(onescol[:, :], 1.0))
    ev_const = P.mop("pool", lambda e: e.memset(epsc[:, :], 1e-5))
    ev_b1 = P.dma("sp", b1sb[:, :], b1r_d[:, :], "s_cst")
    ev_msk = P.dma("sp", masksb[:, :], masks_d[:, :], "s_msk")
    ev_b2 = P.dma("pool", b2row[:, :], b2_d[:, :], "s_cstg")

    ev_z_evac_init = [None]
    ev_xs_user = [None, None, None]
    ev_psT_user = [None, None]

    ev_bn2_user = [None, None]
    ev_ssq_user = [None, None]
    ev_sum4_user = [None, None]
    ev_sq_last = None
    junk_ps = [ps_z[0]]

    def ln_stats(src_evs, par, src_ap):
        nonlocal ev_sq_last
        sp_ = par
        evp = None
        for cc in range(4):
            evp = P.mop("dve", lambda e, c_=cc, s_=src_ap: e.reduce_sum(
                sum4[sp_][:, c_:c_ + 1], s_[:, c_ * 512:(c_ + 1) * 512], axis=AX.X),
                waits=emax(src_evs[cc], ev_sum4_user[sp_]))
        ev_rs = P.mop("dve", lambda e: e.reduce_sum(
            rstd[sp_][:, :], sum4[sp_][:, :], axis=AX.X), waits=emax(evp))
        ev_sum4_user[sp_] = ev_rs
        ev_mean = P.mop("act", lambda e: e.activation(
            bn2[sp_][:, 0:1], rstd[sp_][:, :], AF.Copy, scale=1.0 / 2048),
            waits=emax(ev_rs, ev_bn2_user[sp_]))
        evs = None
        for cc in range(4):
            evs = P.mop("act", lambda e, c_=cc, s_=src_ap, j_=junk_ps[0]: e.activation(
                j_[:, :], s_[:, c_ * 512:(c_ + 1) * 512], AF.Square,
                accum_out=ssq4[sp_][:, c_:c_ + 1]),
                waits=emax(src_evs[cc], ev_ssq_user[sp_], ev_z_evac_init[0]),
                waits_hard=emax(ev_sq_last))
            ev_sq_last = evs
        ev_q = P.mop("dve", lambda e: e.reduce_sum(
            tmp1[sp_][:, :], ssq4[sp_][:, :], axis=AX.X), waits=emax(evs))
        ev_ssq_user[sp_] = ev_q
        ev_msq = P.mop("act", lambda e: e.activation(
            msq[sp_][:, :], bn2[sp_][:, 0:1], AF.Square), waits=emax(ev_mean))
        ev_var = P.mop("dve", lambda e: e.tensor_scalar(
            bn2[sp_][:, 1:2], tmp1[sp_][:, :], 1.0 / 2048, msq[sp_][:, :],
            ALU.mult, ALU.subtract), waits=emax(ev_msq), waits_hard=emax(ev_q))
        ev_sqrt = P.mop("act", lambda e: e.activation(
            rstd[sp_][:, :], bn2[sp_][:, 1:2], AF.Sqrt, bias=epsc[:, :]),
            waits=emax(ev_var, ev_const))
        ev_rcp = P.mop("dve", lambda e: e.reciprocal(rstd[sp_][:, :], rstd[sp_][:, :]),
                       waits=emax(ev_sqrt))
        return ev_rcp

    ev_lnbf_half = [None, None]

    def ln_back(xp, sp_, dst, tile_len, dst_col, ev_rcp):
        nonlocal ev_psT_user
        xs = xstage[xp]
        last_ev = None
        ev_ln = None
        for hf in range(2):
            ev_ln = P.mop("dve", lambda e, h_=hf: e.tensor_scalar(
                lnbf[:, h_ * 1024:(h_ + 1) * 1024],
                xs[:, h_ * 1024:(h_ + 1) * 1024],
                bn2[sp_][:, 0:1], rstd[sp_][:, :],
                ALU.subtract, ALU.mult),
                waits=emax(ev_lnbf_half[hf]), waits_hard=emax(ev_rcp))
            tev = None
            for j in range(8):
                dtile = hf * 8 + j
                tev = P.mop("pe", lambda e, d_=dtile, j_=j, b_=hf: e.transpose(
                    psT[b_][:, j_ * 128:(j_ + 1) * 128],
                    lnbf[:, d_ * 128:(d_ + 1) * 128], id128[:, :]),
                    waits=emax(ev_ln, ev_id, ev_psT_user[hf]))
            ev_lnbf_half[hf] = tev
            g0 = hf * 8
            last_ev = P.mop("dve", lambda e, b_=hf, g0_=g0: e.tensor_copy(
                dst[:, :].rearrange("p (t s) -> p t s", s=tile_len)[
                    :, g0_:g0_ + 8, dst_col:dst_col + 128],
                psT[b_][:, :].rearrange("p (t s) -> p t s", s=128)),
                waits=emax(tev))
            ev_psT_user[hf] = last_ev
        ev_bn2_user[sp_] = ev_ln
        ev_xs_user[xp] = ev_ln
        return last_ev

    ev_wkbuf_user = [None, None, None]
    ev_kvps_evac = [None, None]
    qun = 0
    ev_lnq_half = [None, None]

    def qproj_half(h, sh):
        nonlocal qun
        wpar = qun % 3
        ev_wq = P.dma("pool", wkbuf[wpar][:, :], wq_d[h * 128:(h + 1) * 128, :],
                      f"s_wk{wpar}", waits=emax(ev_wkbuf_user[wpar]))
        pp = qun % 2
        mm = None
        for dt_i in range(DT):
            mm = P.mop("pe", lambda e, d_=dt_i, w_=wpar, p_=pp, s_=sh: e.matmul(
                ps_kv[p_][:, :],
                wkbuf[w_][:, d_ * 128:(d_ + 1) * 128],
                lnq2T[:, d_ * T + s_ * 512: d_ * T + s_ * 512 + 512],
                start=(d_ == 0), stop=(d_ == DT - 1)),
                waits=emax(ev_wq, ev_lnq_half[sh], ev_kvps_evac[pp]))
        qp = qun % 2
        ev_qs = P.mop("act", lambda e, p_=pp, q_=qp: e.activation(
            qstage[q_][:, :], ps_kv[p_][:, :], AF.Copy, scale=SCALE),
            waits=emax(mm, (f"s_qo{qp}", 16 * (qun // 2))))
        ev_kvps_evac[pp] = ev_qs
        P.dma("sp", qT_dram[h * 128:(h + 1) * 128, sh * 512: sh * 512 + 512],
              qstage[qp][:, :], f"s_qo{qp}", waits=emax(ev_qs))
        qun += 1
        ev_wkbuf_user[wpar] = mm

    QSCHED = {4: [(0, 0), (1, 0)], 5: [(2, 0), (3, 0)],
              6: [(4, 0), (5, 0)], 7: [(6, 0), (7, 0)],
              8: [(8, 0), (9, 0)], 9: [(10, 0), (11, 0)],
              10: [(12, 0), (13, 0)], 11: [(14, 0), (15, 0)],
              12: [(0, 1), (1, 1)], 13: [(2, 1), (3, 1)],
              14: [(4, 1), (5, 1)], 15: [(6, 1), (7, 1)],
              16: [(8, 1)], 17: [(9, 1)], 18: [(10, 1)], 19: [(11, 1)],
              20: [(12, 1)], 21: [(13, 1)], 22: [(14, 1)], 23: [(15, 1)]}

    tiles = [(xq_d, k * 128, lnq2T, T, k * 128) for k in range(NS)] + \
            [(xb_d, i * 128, lnxT, SEQ, i * 128) for i in range(SEQ // 128)]
    NT = 1 if stage == "A1" else len(tiles)

    def ln_load(t):
        dram, r0, _, _, _ = tiles[t]
        p3 = t % 3
        return P.dma("sp", xstage[p3][:, :], dram[r0:r0 + 128, :],
                     f"s_x{p3}", waits=emax(ev_xs_user[p3]))

    def ln_front(t, ev_load):
        return ln_stats([ev_load] * 4, t % 2, xstage[t % 3][:, :])

    ev_lnq_done = None
    ev_lnx_done = None
    loads = {0: ln_load(0)}
    if NT > 1:
        loads[1] = ln_load(1)
    rcp_evs = {0: ln_front(0, loads.pop(0))}
    for t in range(NT):
        if t + 2 < NT:
            loads[t + 2] = ln_load(t + 2)
        if t + 1 < NT:
            rcp_evs[t + 1] = ln_front(t + 1, loads.pop(t + 1))
        _, _, dst, tl, dc = tiles[t]
        ev = ln_back(t % 3, t % 2, dst, tl, dc, rcp_evs.pop(t))
        if t == 3:
            ev_lnq_half[0] = ev
        if t == NS - 1:
            ev_lnq_done = ev
        ev_lnx_done = ev
        if stage != "A1":
            for (h_, sh_) in QSCHED.get(t, []):
                qproj_half(h_, sh_)
    ev_lnq_half[1] = ev_lnq_done
    if stage == "A1":
        P.dma("sp", dbg_bn[:, :], bn2[0][:, :], "s_dbg", waits=emax(ev_lnx_done))
        P.dma("sp", dbg_lnbf[:, :], lnbf[:, :], "s_dbg", waits=emax(ev_lnx_done))
        ev_d = P.dma("sp", dbg_lnqT[:, :], lnq2T[:, :], "s_dbg", waits=emax(ev_lnx_done))
        return finish(emax(("s_dbg", P.count.get("s_dbg", 0)), ev_d))

    if stage == "A":
        P.dma("sp", dbg_lnqT[:, :], lnq2T[:, :], "s_dbg", waits=emax(ev_lnq_done))
        ev_d = P.dma("sp", dbg_lnxT[:, :], lnxT[:, :], "s_dbg", waits=emax(ev_lnx_done))
        return finish(emax(ev_d, ("s_qo0", 16 * 16), ("s_qo1", 16 * 16)))

    p1_chunks = []
    for c in range(64):
        rt, ch = divmod(c, 4)
        p1_chunks.append((w1t_d, rt * 128, ch * 2048, part1, c))
    for c in range(64):
        p1_chunks.append((w2t_d, c * 128, 0, part2, c))
    p1_state = {"dma": 0, "abs": 0, "red": [None, None],
                "partev": [None, None], "evl": {}}

    def p1_dma():
        i = p1_state["dma"]
        if i >= len(p1_chunks):
            return
        dram, r0, c0, partsb, col = p1_chunks[i]
        par = i % 2
        p1_state["evl"][i] = P.dma(
            "sp", xstage[par][:, :], dram[r0:r0 + 128, c0:c0 + 2048],
            f"s_x{par}", waits=emax(p1_state["red"][par], ev_xs_user[par]))
        ev_xs_user[par] = None
        p1_state["dma"] = i + 1

    def p1_abs():
        j = p1_state["abs"]
        if j >= 4 * min(p1_state["dma"], len(p1_chunks)):
            return
        i, sc = divmod(j, 4)
        dram, r0, c0, partsb, col = p1_chunks[i]
        par = i % 2
        ev_r = P.mop("dve", lambda e, sb_=partsb, c_=col, p_=par, s_=sc:
                     e.tensor_reduce(
            sb_[:, c_ * 4 + s_: c_ * 4 + s_ + 1],
            xstage[p_][:, s_ * 512:(s_ + 1) * 512],
            axis=AX.X, op=ALU.add, apply_absolute_value=True),
            waits=emax(p1_state["evl"][i] if sc == 0 else None))
        if sc == 3:
            p1_state["evl"].pop(i, None)
            p1_state["red"][par] = ev_r
            p1_state["partev"][0 if partsb is part1 else 1] = ev_r
        p1_state["abs"] = j + 1

    def pump_pass1(n):
        for _ in range(n):
            if p1_state["dma"] - p1_state["abs"] // 4 >= 2 or \
               p1_state["dma"] >= len(p1_chunks):
                p1_abs()
            if p1_state["abs"] % 4 == 0 or p1_state["dma"] - p1_state["abs"] // 4 < 2:
                p1_dma()
        while p1_state["dma"] >= len(p1_chunks) and n >= 4 * len(p1_chunks) and \
                p1_state["abs"] < 4 * len(p1_chunks):
            p1_abs()

    ev_qTq_user = None
    ev_zTq_out = None
    ev_zevac_last = None
    ev_wvq_user = None
    ev_masksb_user = None
    ev_kTb_user = None
    ev_kv_k = {}
    ev_kv_v = {}
    ev_S_user = [None] * NSB
    ev_patT_user = [None, None]
    ev_sc_evac = [None, None]
    ev_z_evac = [ev_sq_last, None]
    ev_patTe_last = None
    uidx = 0
    scp = 0

    for g in range(NQ):
        ev_qTq_load = P.dma(
            "sp", qTq[:, :].rearrange("p (h s) -> p h s", s=T),
            qT_dram[g * 512:(g + 1) * 512, :].rearrange("(h e) s -> e h s", e=128),
            "s_qTq",
            waits=emax(ev_qTq_user, ("s_qo0", 16 * 16), ("s_qo1", 16 * 16)))

        ev_wv = P.dma("pool", wvq[:, :], wv_d[g * 128:(g + 1) * 128, :],
                      "s_wv", waits=emax(ev_wvq_user))
        kvp = 0
        for hc in range(HPQ):
            h = g * HPQ + hc
            wpar = hc % 3
            ev_wk = P.dma("pool", wkbuf[wpar][:, :], wk_d[h * 128:(h + 1) * 128, :],
                          f"s_wk{wpar}", waits=emax(ev_wkbuf_user[wpar]))
            mm = None
            for mb in range(4):
                pp = kvp % 2
                kvp += 1
                for dt_i in range(DT):
                    mm = P.mop("pe", lambda e, d_=dt_i, w_=wpar, p_=pp, m_=mb: e.matmul(
                        ps_kv[p_][:, :],
                        wkbuf[w_][:, d_ * 128:(d_ + 1) * 128],
                        lnxT[:, d_ * SEQ + m_ * 512: d_ * SEQ + m_ * 512 + 512],
                        start=(d_ == 0), stop=(d_ == DT - 1)),
                        waits=emax(ev_wk, ev_lnx_done, ev_kvps_evac[pp]))
                ev = P.mop("act", lambda e, p_=pp, hc_=hc, m_=mb: e.activation(
                    kTb[:, hc_ * SEQ + m_ * 512: hc_ * SEQ + m_ * 512 + 512],
                    ps_kv[p_][:, :], AF.Copy), waits=emax(mm))
                ev_kvps_evac[pp] = ev
                pump_pass1(3)
            ev_kv_k[(g, hc)] = ev
            ev_wkbuf_user[wpar] = mm
        for j in range(16):
            pp = kvp % 2
            kvp += 1
            mm = None
            for dt_i in range(DT):
                mm = P.mop("pe", lambda e, d_=dt_i, p_=pp, j_=j: e.matmul(
                    ps_kv[p_][:, :],
                    lnxT[:, d_ * SEQ + j_ * 128: d_ * SEQ + j_ * 128 + 128],
                    wvq[:, d_ * 512:(d_ + 1) * 512],
                    start=(d_ == 0), stop=(d_ == DT - 1)),
                    waits=emax(ev_wv, ev_lnx_done, ev_kvps_evac[pp]))
            ev = P.mop("act", lambda e, p_=pp, j_=j: e.activation(
                vqb[:, j_ * 512:(j_ + 1) * 512], ps_kv[p_][:, :], AF.Copy),
                waits=emax(mm))
            ev_kvps_evac[pp] = ev
            ev_kv_v[(g, j)] = ev
            pump_pass1(3)
        ev_wvq_user = mm

        def unit_front(k, hc):
            nonlocal uidx, scp, ev_qTq_user, ev_masksb_user, ev_kTb_user
            ext = EXT[k]
            W = ext * 128
            nch = (ext + 3) // 4
            par = uidx % NSB
            u_g = uidx
            uidx += 1
            ev_e = None
            for c in range(nch):
                wc = min(512, W - c * 512)
                pp = scp % 2
                scp += 1
                last = c == nch - 1
                mm = P.mop("pe", lambda e, p_=pp, c_=c, w_=wc, k_=k, h_=hc, l_=last:
                           e.matmul(
                    ps_sc[p_][:, 0:w_],
                    qTq[:, h_ * T + k_ * 128: h_ * T + k_ * 128 + 128],
                    kTb[:, h_ * SEQ + c_ * 512: h_ * SEQ + c_ * 512 + w_],
                    start=True, stop=not l_),
                    waits=emax(ev_qTq_load, ev_kv_k[(g, hc)], ev_sc_evac[pp]))
                ev_qTq_user = mm
                ev_kTb_user = mm
                if last:
                    moff = wc - 256
                    mm = P.mop("pe", lambda e, p_=pp, mo_=moff, k_=k: e.matmul(
                        ps_sc[p_][:, mo_:mo_ + 256], id128[:, :],
                        masksb[:, k_ * 256:(k_ + 1) * 256],
                        start=False, stop=True),
                        waits=emax(ev_msk, ev_id))
                    ev_masksb_user = mm
                c0 = c * 512
                ev_e = P.mop("act", lambda e, p_=pp, q_=par, c0_=c0, w_=wc, c_=c:
                             e.activation(Sbuf[q_][:, c0_:c0_ + w_],
                                          ps_sc[p_][:, 0:w_], AF.Exp,
                                          accum_out=sume4[q_][:, c_:c_ + 1]),
                             waits=emax(mm, ev_S_user[par]))
                ev_sc_evac[pp] = ev_e
            ev_red = P.mop("dve", lambda e, q_=par, n_=nch: e.reduce_sum(
                sume[q_][:, :], sume4[q_][:, 0:n_], axis=AX.X),
                waits=emax(ev_e))
            ev_rcp = P.mop("dve", lambda e, q_=par: e.reciprocal(rs[q_][:, :], sume[q_][:, :]),
                           waits=emax(ev_red))
            ev_dg = P.mop("dve", lambda e, q_=par, w_=W: e.tensor_scalar(
                Sbuf[q_][:, 0:w_], Sbuf[q_][:, 0:w_], rs[q_][:, :], None, ALU.mult),
                waits=emax(ev_id), waits_hard=emax(ev_rcp))
            return {"k": k, "hc": hc, "par": par, "ext": ext, "ev_dg": ev_dg,
                    "u": u_g}

        def unit_back(st):
            nonlocal ev_psT_user, ev_patT_user, ev_patTe_last
            nonlocal ev_zevac_last
            k, hc, par, ext = st["k"], st["hc"], st["par"], st["ext"]
            ev_dg = st["ev_dg"]
            zpp = st["u"] % 2
            up = st["u"] % 2
            ngrp = (ext + 7) // 8
            grp_ev = []
            tev = None
            for grp in range(ngrp):
                bank = grp % 2
                n_in = min(8, ext - grp * 8)
                for j in range(n_in):
                    mt = grp * 8 + j
                    tev = P.mop("pe", lambda e, b_=bank, j_=j, m_=mt, q_=par: e.transpose(
                        psT[b_][:, j_ * 128:(j_ + 1) * 128],
                        Sbuf[q_][:, m_ * 128:(m_ + 1) * 128], id128[:, :]),
                        waits=emax(ev_dg, ev_psT_user[bank]))
                pev = P.mop("dve", lambda e, b_=bank, g_=grp, n_=n_in, u_=up: e.tensor_copy(
                    patT[u_][:, g_ * 1024: g_ * 1024 + n_ * 128],
                    psT[b_][:, 0:n_ * 128]),
                    waits=emax(tev, ev_patT_user[up]))
                ev_psT_user[bank] = pev
                ev_patTe_last = pev
                grp_ev.append(pev)
            ev_S_user[par] = tev
            mm = None
            for mt in range(ext):
                mm = P.mop("pe", lambda e, p_=zpp, m_=mt, h_=hc, x_=ext, u_=up: e.matmul(
                    ps_z[p_][:, 0:128],
                    vqb[:, m_ * 512 + h_ * 128: m_ * 512 + h_ * 128 + 128],
                    patT[u_][:, m_ * 128:(m_ + 1) * 128],
                    start=(m_ == 0), stop=(m_ == x_ - 1)),
                    waits=emax(grp_ev[mt // 8], ev_kv_v[(g, mt)], ev_z_evac[zpp]))
            ev_patT_user[up] = mm
            ev_zevac_last = P.mop("act", lambda e, p_=zpp, h_=hc, k_=k: e.activation(
                zTq[:, h_ * T + k_ * 128: h_ * T + k_ * 128 + 128],
                ps_z[p_][:, 0:128], AF.Copy),
                waits=emax(mm, ("s_zo", 16 * NS * g)))
            ev_z_evac[zpp] = ev_zevac_last

        def flush_back(st):
            nonlocal ev_zTq_out
            unit_back(st)
            if st["hc"] == HPQ - 1:
                k = st["k"]
                ev_zTq_out = P.dma(
                    "sp",
                    zT_dram[k * 128:(k + 1) * 128,
                            g * 512:(g + 1) * 512].rearrange("e (h s) -> e h s", s=128),
                    zTq[:, :].rearrange("p (h q) -> p h q", q=T)[:, :, k * 128:(k + 1) * 128],
                    "s_zo", waits=emax(ev_zevac_last))

        pend = []
        for k in range(NS):
            for hc in range(HPQ):
                st = unit_front(k, hc)
                pump_pass1(1)
                pend.append(st)
                if len(pend) >= 4:
                    flush_back(pend.pop(0))
        while pend:
            flush_back(pend.pop(0))
    pump_pass1(4 * len(p1_chunks))

    if stage == "B":
        P.dma("sp", dbg_S[:, :], Sbuf[1][:, :], "s_dbg", waits=emax(ev_S_user[1], ev_patT_user))
        P.dma("sp", dbg_pat[:, :], pat[1][:, :], "s_dbg", waits=emax(ev_patT_user))
        P.dma("sp", dbg_patT[:, :], patT[:, :], "s_dbg", waits=emax(ev_patT_user))
        evsm = P.mop("dve", lambda e: e.tensor_copy(ssq4[0][:, 0:1], sume4[1][:, 0:1]),
                     waits=emax(ev_patT_user))
        evsm = P.mop("dve", lambda e: e.tensor_copy(ssq4[0][:, 1:2], sume[1][:, :]))
        evsm = P.mop("dve", lambda e: e.tensor_copy(ssq4[0][:, 2:3], rs[1][:, :]))
        P.dma("sp", dbg_nm[:, :], ssq4[0][:, :], "s_dbg", waits=emax(evsm))
        P.dma("sp", dbg_kT[:, :], kTb[:, :], "s_dbg", waits=emax(ev_kv_k[(3, 3)]))
        P.dma("sp", dbg_v[:, :], vqb[:, :], "s_dbg", waits=emax(ev_kv_v[(3, 15)]))
        ev_pc = P.mop("dve", lambda e: e.tensor_copy(
            xstage[0][:, 0:64], part1[:, :]),
            waits=emax(p1_state["red"][0], p1_state["red"][1]))
        ev_pc2 = P.mop("dve", lambda e: e.tensor_copy(xstage[0][:, 64:128], part2[:, :]))
        ev_pd = P.dma("sp", dbg_parts[:, :], xstage[0][:, 0:128], "s_dbg",
                      waits=emax(ev_pc2))
        return finish(emax(ev_pd, ("s_zo", 16 * NS * NQ), ("s_dbg", P.count.get("s_dbg", 0))))

    ev_att_done = emax(ev_patT_user[0], ev_patT_user[1], ev_qTq_user,
                       ev_zevac_last, ev_patTe_last,
                       ("s_zo", 16 * NS * NQ), ("s_qo0", 16 * 16), ("s_qo1", 16 * 16),
                       *ev_S_user)

    mmp1 = P.mop("pe", lambda e: e.matmul(
        ps_kv[0][0:1, 0:256], onescol[:, :], part1[:, :], start=True, stop=True),
        waits=emax(ev_onescol, p1_state["partev"][0], ev_kvps_evac[0]))
    mmp2 = P.mop("pe", lambda e: e.matmul(
        ps_kv[1][0:1, 0:256], onescol[:, :], part2[:, :], start=True, stop=True),
        waits=emax(p1_state["partev"][1], ev_kvps_evac[1]))
    ev_s1 = P.mop("dve", lambda e: e.tensor_reduce(
        scal2[0:1, 0:1], ps_kv[0][0:1, 0:256], axis=AX.X, op=ALU.add),
        waits=emax(mmp1))
    ev_s2 = P.mop("dve", lambda e: e.tensor_reduce(
        scal2[0:1, 1:2], ps_kv[1][0:1, 0:256], axis=AX.X, op=ALU.add),
        waits=emax(mmp2))
    CQ = 0.7 / float(FF * D)
    ev_t1s = P.mop("act", lambda e: e.activation(
        scal2[0:1, 0:1], scal2[0:1, 0:1], AF.Copy, scale=CQ), waits=emax(ev_s1))
    ev_t2s = P.mop("act", lambda e: e.activation(
        scal2[0:1, 1:2], scal2[0:1, 1:2], AF.Copy, scale=CQ), waits=emax(ev_s2))

    def thresh_back():
        mm1 = P.mop("pe", lambda e: e.matmul(ps_kv[0][:, 0:1], ones_f[:, :],
                                             scal2[0:1, 0:1], start=True, stop=True),
                    waits=emax(ev_t1s, ev_onesf, ev_kvps_evac[0]))
        mm2 = P.mop("pe", lambda e: e.matmul(ps_kv[1][:, 0:1], ones_f[:, :],
                                             scal2[0:1, 1:2], start=True, stop=True),
                    waits=emax(ev_t2s, ev_kvps_evac[1]))
        e1 = P.mop("dve", lambda e: e.tensor_copy(t1b[:, :], ps_kv[0][:, 0:1]),
                   waits=emax(mm1))
        e2 = P.mop("dve", lambda e: e.tensor_copy(t2b[:, :], ps_kv[1][:, 0:1]),
                   waits=emax(mm2))
        e3 = P.mop("dve", lambda e: e.tensor_scalar(
            negt1b[:, :], t1b[:, :], -1.0, None, ALU.mult), waits=emax(e1))
        e4 = P.mop("dve", lambda e: e.tensor_scalar(
            negt2b[:, :], t2b[:, :], -1.0, None, ALU.mult), waits=emax(e2))
        return e1, e2, e3, e4

    ev_qst1_user = [emax(ev_masksb_user), emax(p1_state["red"][0])]
    ev_qab1_user = [emax(p1_state["red"][0]), emax(p1_state["red"][0])]
    ev_qst2_user = [emax(p1_state["red"][1]), emax(p1_state["red"][1])]
    ev_qsg2_user = [emax(p1_state["red"][1]), emax(p1_state["red"][1])]
    ev_w1q_ready = [None, None]
    ev_w2q_ready = [None, None]
    ev_w1p = {}
    ev_w2p = {}
    ev_w1q_user = [None, None]
    ev_w2q_user = [None, None]

    def quant_w1_piece(c, p):
        par = p % 2
        src_ap = w1t_d[p * 256:(p + 1) * 256,
                       c * FCW:(c + 1) * FCW].rearrange("(t q) f -> q t f", q=128)
        ev_l = P.dma("sp", qst1[par][:, :].rearrange("q (t f) -> q t f", f=512),
                     src_ap, f"s_w1{par}", waits=emax(ev_qst1_user[par]))
        ev_b = P.mop("dve", lambda e, p_=par: e.tensor_scalar(
            qab1[p_][:, :], qst1[p_][:, :], negt1b[:, :], None, ALU.is_le),
            waits=emax(ev_l, ev_qab1_user[par]), waits_hard=emax(ev_negt))
        dcol = 2 * p * FCW
        last = P.mop("dve", lambda e, p_=par, dc_=dcol, c_=c: e.scalar_tensor_tensor(
            w1q[c_ % 2][:, dc_:dc_ + 1024], qst1[p_][:, :], t1b[:, :],
            qab1[p_][:, :], ALU.is_ge, ALU.subtract),
            waits=emax(ev_w1q_user[c % 2]), waits_hard=emax(ev_b, ev_t1b))
        ev_qst1_user[par] = last
        ev_qab1_user[par] = last
        ev_w1p[(c, p)] = last
        if p == 7:
            ev_w1q_ready[c % 2] = last

    def quant_w2_piece(c, p, par=None):
        par = (p % 2) if par is None else par
        rt, hf = divmod(p, 4)
        src_ap = w2t_d[c * FCW + rt * 128: c * FCW + (rt + 1) * 128,
                       hf * 512:(hf + 1) * 512]
        ev_l = P.dma("pool", qst2[par][:, :], src_ap,
                     f"s_w2{par}", waits=emax(ev_qst2_user[par]))
        ev_sg = P.mop("dve", lambda e, p_=par: e.tensor_scalar(
            qsg2[p_][:, :], qst2[p_][:, :], negt2b[:, :], None, ALU.is_le),
            waits=emax(ev_l, ev_qsg2_user[par]), waits_hard=emax(ev_negt2))
        dcol = rt * D + hf * 512
        last = P.mop("dve", lambda e, p_=par, dc_=dcol, c_=c: e.scalar_tensor_tensor(
            w2q[c_ % 2][:, dc_:dc_ + 512], qst2[p_][:, :], t2b[:, :],
            qsg2[p_][:, :], ALU.is_ge, ALU.subtract),
            waits=emax(ev_w2q_user[c % 2]), waits_hard=emax(ev_sg, ev_t2b))
        ev_qst2_user[par] = last
        ev_qsg2_user[par] = last
        ev_w2p[(c, p)] = last
        if p == 15:
            ev_w2q_ready[c % 2] = last

    ev_x1_load = P.dma("act", x1[:, :].rearrange("p (s d) -> p s d", d=D),
                       xq_d[:, :].rearrange("(s p) d -> p s d", p=128),
                       "s_x1in", waits=ev_att_done)
    ev_zl = {}
    for k in range(NS):
        ev_zl[k] = P.dma("sp", zTop[k][:, :], zT_dram[k * 128:(k + 1) * 128, :],
                         "s_zi0", waits=[("s_zo", 16 * NS * NQ)])
    opsum = [ps_sc[0], ps_sc[1], ps_z[0], ps_z[1]]
    opev = [ev_sc_evac[0], ev_sc_evac[1], ev_z_evac[0], ev_z_evac[1]]
    ev_wo_user = [None, None]
    ev_x1_evac = {}
    opp = 0
    mm = None
    ev_wo_dm2 = None
    for dm in range(4):
        wpar = dm % 2
        if dm == 0:
            wo_w = emax(ev_wvq_user, ev_kTb_user)
        elif dm == 1:
            wo_w = emax(ev_kTb_user, ev_patT_user[0], ev_patT_user[1])
        else:
            wo_w = emax(ev_wo_user[wpar])
        ev_wo = P.dma("pool", wobuf[wpar][:, :], wo_d[dm * 128:(dm + 1) * 128, :],
                      f"s_wo{wpar}", waits=wo_w)
        if dm == 3:
            ev_w1q_user[0] = ev_wo_dm2
        for k in range(NS):
            pp = opp % 4
            opp += 1
            for ht in range(DT):
                mm = P.mop("pe", lambda e, p_=pp, t_=ht, k_=k, w_=wpar: e.matmul(
                    opsum[p_][:, :],
                    zTop[k_][:, t_ * 128:(t_ + 1) * 128],
                    wobuf[w_][:, t_ * 512:(t_ + 1) * 512],
                    start=(t_ == 0), stop=(t_ == DT - 1)),
                    waits=emax(ev_zl[k], ev_wo, opev[pp]))
            ev = P.mop("dve", lambda e, p_=pp, k_=k, d_=dm: e.tensor_tensor(
                x1[:, k_ * D + d_ * 512: k_ * D + d_ * 512 + 512],
                x1[:, k_ * D + d_ * 512: k_ * D + d_ * 512 + 512],
                opsum[p_][:, :], ALU.add),
                waits=emax(mm, ev_x1_load))
            opev[pp] = ev
            ev_x1_evac[(dm, k)] = ev
            if dm == 3:
                quant_w1_piece(0, k)
        ev_wo_user[wpar] = mm
        if dm == 2:
            ev_wo_dm2 = mm
            ev_t1b, ev_t2b, ev_negt, ev_negt2 = thresh_back()
    ev_oproj_pe_done = mm
    ev_sc_evac[0] = opev[0]
    ev_sc_evac[1] = opev[1]
    ev_w1q_user[1] = ev_oproj_pe_done
    ev_w2q_user[0] = ev_oproj_pe_done
    ev_w2q_user[1] = ev_oproj_pe_done

    junk_ps[0] = ps_kv[0]
    ev_z_evac_init[0] = ev_t1b
    ev_ln2_done = None

    def ln2_back(k, ev_rcp):
        nonlocal ev_ln2_done
        sp_ = k % 2
        for hf in range(2):
            ev_ln = P.mop("dve", lambda e, k_=k, s_=sp_, h_=hf: e.tensor_scalar(
                lnbf[:, h_ * 1024:(h_ + 1) * 1024],
                x1[:, k_ * D + h_ * 1024: k_ * D + h_ * 1024 + 1024],
                bn2[s_][:, 0:1], rstd[s_][:, :],
                ALU.subtract, ALU.mult),
                waits=emax(ev_lnbf_half[hf]), waits_hard=emax(ev_rcp))
            tev = None
            for j in range(8):
                dtile = hf * 8 + j
                tev = P.mop("pe", lambda e, d_=dtile, j_=j, b_=hf: e.transpose(
                    psT[b_][:, j_ * 128:(j_ + 1) * 128],
                    lnbf[:, d_ * 128:(d_ + 1) * 128], id128[:, :]),
                    waits=emax(ev_ln, ev_psT_user[hf]))
            ev_lnbf_half[hf] = tev
            g0 = hf * 8
            ev_ln2_done = P.mop("dve", lambda e, b_=hf, g0_=g0, k_=k: e.tensor_copy(
                ln2T[:, :].rearrange("p (t s) -> p t s", s=T)[
                    :, g0_:g0_ + 8, k_ * 128:(k_ + 1) * 128],
                psT[b_][:, :].rearrange("p (t s) -> p t s", s=128)),
                waits=emax(tev))
            ev_psT_user[hf] = ev_ln2_done
            ev_bn2_user[sp_] = ev_ln

    def ln2_front(k):
        return ln_stats([ev_x1_evac[(c, k)] for c in range(4)], k % 2,
                        x1[:, k * D:(k + 1) * D])

    W2ORD = [dm_ + 4 * ft_ for dm_ in range(4) for ft_ in range(4)]
    ln2_rcp = {0: ln2_front(0)}
    for k in range(NS):
        if k + 1 < NS:
            ln2_rcp[k + 1] = ln2_front(k + 1)
        ln2_back(k, ln2_rcp.pop(k))
        quant_w2_piece(0, W2ORD[2 * k], par=0)
        quant_w2_piece(0, W2ORD[2 * k + 1], par=1)

    if stage == "C":
        P.mop("dve", lambda e: e.tensor_copy(bn2[0][:, 0:1], t1b[:, :]), waits=emax(ev_t1b))
        ev_dbt = P.mop("dve", lambda e: e.tensor_copy(bn2[0][:, 1:2], t2b[:, :]), waits=emax(ev_t2b))
        P.dma("sp", dbg_t[:, :], bn2[0][:, :], "s_dbg", waits=emax(ev_dbt))
        P.dma("sp", dbg_x1[:, :].rearrange("(s p) d -> p s d", p=128),
              x1[:, :].rearrange("p (s d) -> p s d", d=D),
              "s_dbg", waits=emax(*[ev_x1_evac[(dm, k)] for dm in range(4) for k in range(NS)]))
        ev_d2 = P.dma("sp", dbg_ln2T[:, :], ln2T[:, :], "s_dbg", waits=emax(ev_ln2_done))
        return finish([("s_dbg", P.count.get("s_dbg", 0))])

    ev_gelu = {}
    psF2 = [ps_z[0], ps_z[1], ps_kv[0], ps_kv[1]]
    ev_f2 = [opev[2], opev[3], ev_sq_last, ev_t2b]
    ffp1 = 0
    ffp2 = 0
    ev_ffn_out = []
    for c in range(NFC):
        for fs in range(4):
            fi = c * 4 + fs
            for tb_i in range(2):
                pp = ffp1 % 2
                ffp1 += 1
                mm = None
                for dt_i in range(DT):
                    mm = P.mop("pe", lambda e, d_=dt_i, p_=pp, c_=c, f_=fs, t_=tb_i: e.matmul(
                        ps_sc[p_][:, :],
                        w1q[c_ % 2][:, d_ * FCW + f_ * 128: d_ * FCW + f_ * 128 + 128],
                        ln2T[:, d_ * T + t_ * 512: d_ * T + t_ * 512 + 512],
                        start=(d_ == 0), stop=(d_ == DT - 1)),
                        waits=emax(ev_w1p[(c, dt_i // 2)], ev_ln2_done, ev_sc_evac[pp]))
                ev_g = P.mop("act", lambda e, p_=pp, f_=fs, t_=tb_i, fi_=fi: e.activation(
                    gTb[:, f_ * T + t_ * 512: f_ * T + t_ * 512 + 512],
                    ps_sc[p_][:, :], AF.Gelu, bias=b1sb[:, fi_:fi_ + 1]),
                    waits=emax(mm, ev_b1))
                ev_sc_evac[pp] = ev_g
                ev_gelu[(c, fs, tb_i)] = ev_g
            if c + 1 < NFC:
                quant_w1_piece(c + 1, 2 * fs)
                quant_w1_piece(c + 1, 2 * fs + 1)
        ev_w1q_user[c % 2] = mm
        for tc in range(NS):
            for dm in range(4):
                pp = ffp2 % 4
                ffp2 += 1
                mm = None
                for ft in range(4):
                    mm = P.mop("pe", lambda e, p_=pp, f_=ft, t_=tc, d_=dm, c_=c: e.matmul(
                        psF2[p_][:, :],
                        gTb[:, f_ * T + t_ * 128: f_ * T + t_ * 128 + 128],
                        w2q[c_ % 2][:, f_ * D + d_ * 512: f_ * D + d_ * 512 + 512],
                        start=(f_ == 0), stop=(f_ == 3 and c_ != 0)),
                        waits=emax(ev_gelu[(c, ft, tc // 4)],
                                   ev_w2p[(c, ft * 4 + dm)], ev_f2[pp]))
                if c == 0:
                    mm = P.mop("pe", lambda e, p_=pp, d_=dm: e.matmul(
                        psF2[p_][:, :], ones_bf[:, :],
                        b2row[0:1, d_ * 512:(d_ + 1) * 512],
                        start=False, stop=True),
                        waits=emax(ev_b2, ev_ones))
                ev = P.mop("dve", lambda e, t_=tc, d_=dm, p_=pp: e.tensor_tensor(
                    x1[:, t_ * D + d_ * 512: t_ * D + d_ * 512 + 512],
                    x1[:, t_ * D + d_ * 512: t_ * D + d_ * 512 + 512],
                    psF2[p_][:, :], ALU.add),
                    waits=emax(mm))
                ev_f2[pp] = ev
                if c == NFC - 1:
                    ev_ffn_out.append(ev)
            if c + 1 < NFC:
                quant_w2_piece(c + 1, 2 * tc)
                quant_w2_piece(c + 1, 2 * tc + 1)
        ev_w2q_user[c % 2] = mm

    P.dma("sp", out_d[:, :].rearrange("(s p) d -> p s d", p=128),
          x1[:, :].rearrange("p (s d) -> p s d", d=D),
          "s_out", waits=emax(*ev_ffn_out))
    if stage == "F":
        P.dma("sp", dbg_w1q[:, :], w1q[(NFC - 1) % 2][:, :], "s_dbg",
              waits=emax(*ev_ffn_out))
        P.dma("sp", dbg_w2q[:, :], w2q[(NFC - 1) % 2][:, :], "s_dbg")
        P.dma("sp", dbg_gT[:, :], gTb[:, :], "s_dbg")
        P.mop("dve", lambda e: e.tensor_copy(bn2[0][:, 0:1], t1b[:, :]))
        ev_dbt = P.mop("dve", lambda e: e.tensor_copy(bn2[0][:, 1:2], t2b[:, :]))
        P.dma("sp", dbg_t[:, :], bn2[0][:, :], "s_dbg", waits=emax(ev_dbt))
        return finish([("s_out", 16), ("s_dbg", P.count.get("s_dbg", 0))])
    return finish([("s_out", 16)])


def _hm(wT, nb, eb):
    return np.ascontiguousarray(
        wT.reshape(DT, 128, nb, eb).transpose(2, 1, 0, 3).reshape(nb * 128, DT * eb))


def _build_masks(blocks):
    ets = [b + 1 for b in blocks]
    m = np.zeros((128, NS * 256), np.float32)
    tri = np.triu(np.ones((128, 128), dtype=bool), 1)
    for k in range(NS):
        Ek, et = EXT[k], ets[k]
        assert et <= Ek
        for t in range(2):
            jj = Ek - 2 + t
            w = m[:, k * 256 + t * 128: k * 256 + (t + 1) * 128]
            if jj == et - 1:
                w[tri] = -1e9
            elif jj > et - 1:
                w[:, :] = -1e9
    return m


_CACHE = {}


def kernel(**inputs):
    x = np.ascontiguousarray(np.asarray(inputs["L0_attn_x"], dtype=np.float32))
    WQ = np.asarray(inputs["L0_attn_W_Q"], dtype=np.float32)
    WK = np.asarray(inputs["L0_attn_W_K"], dtype=np.float32)
    WV = np.asarray(inputs["L0_attn_W_V"], dtype=np.float32)
    WO = np.asarray(inputs["L0_attn_W_O"], dtype=np.float32)
    w1 = np.asarray(inputs["L0_ffn_w1"], dtype=np.float32)
    b1 = np.asarray(inputs["L0_ffn_b1"], dtype=np.float32)
    w2 = np.asarray(inputs["L0_ffn_w2"], dtype=np.float32)
    b2 = np.asarray(inputs["L0_ffn_b2"], dtype=np.float32)

    wqT = _hm(WQ.transpose(2, 0, 1).reshape(D, D), H, 128)
    wkT = _hm(WK.transpose(2, 0, 1).reshape(D, D), H, 128)
    wvT = _hm(WV.transpose(2, 0, 1).reshape(D, D), NQ, 512)
    woT = _hm(WO.transpose(0, 2, 1).reshape(D, D), 4, 512)
    w1t = np.ascontiguousarray(w1.T)
    w2t = np.ascontiguousarray(w2.T)
    b1r = np.ascontiguousarray(b1.reshape(64, 128).T)
    b2r = np.ascontiguousarray(b2.reshape(1, D))
    masks_e = _build_masks(BLOCKS_EVEN).astype(BF)
    masks_o = _build_masks(BLOCKS_ODD).astype(BF)

    if "prog" not in _CACHE:
        _CACHE["prog"] = build_program(debug=False)
    nc, _ = _CACHE["prog"]

    in_maps = []
    for core in range(8):
        b = core // 2
        blocks = BLOCKS_EVEN if core % 2 == 0 else BLOCKS_ODD
        xb = np.ascontiguousarray(x[b])
        xq = np.ascontiguousarray(
            np.concatenate([x[b][blk * 128:(blk + 1) * 128] for blk in blocks], axis=0))
        in_maps.append({
            "xb": xb, "xq": xq, "wq": wqT, "wk": wkT, "wv": wvT, "wo": woT,
            "w1t": w1t, "w2t": w2t, "b1r": b1r, "b2": b2r,
            "masks": masks_e if core % 2 == 0 else masks_o,
        })

    res = run_bass_kernel_spmd(nc, in_maps, core_ids=list(range(8)))

    out = np.empty((4, SEQ, D), dtype=np.float32)
    for core in range(8):
        b = core // 2
        blocks = BLOCKS_EVEN if core % 2 == 0 else BLOCKS_ODD
        loc = np.asarray(res.results[core]["out"])
        for k, blk in enumerate(blocks):
            out[b, blk * 128:(blk + 1) * 128] = loc[k * 128:(k + 1) * 128]
    return out


# version 46
# speedup vs baseline: 1.0347x; 1.0347x over previous
import numpy as np
import ml_dtypes

import concourse.bass as bass
import concourse.mybir as mybir
from concourse.bass_utils import run_bass_kernel_spmd

F32 = mybir.dt.float32
BF16 = mybir.dt.bfloat16
AF = mybir.ActivationFunctionType
ALU = mybir.AluOpType
AX = mybir.AxisListType
BF = ml_dtypes.bfloat16

D = 2048
SEQ = 2048
H = 16
E = 128
T = 1024
NS = 8
FF = 8192
DT = D // 128
EXT = [2, 4, 6, 8, 10, 12, 14, 16]
NQ = 4
HPQ = H // NQ
SCALE = float(1.0 / np.sqrt(128.0))
NFC = 16
FCW = FF // NFC

BLOCKS_EVEN = [0, 3, 4, 7, 8, 11, 12, 15]
BLOCKS_ODD = [1, 2, 5, 6, 9, 10, 13, 14]
NSB = 5

ENGINES = ("pe", "act", "dve", "pool", "sp")
_BLOCK_ATTR = {"pe": "tensor", "act": "scalar", "dve": "vector",
               "pool": "gpsimd", "sp": "sync"}


class Plan:

    def __init__(self, nc, serial=False):
        self.nc = nc
        self.serial = serial
        self.items = {e: [] for e in ENGINES}
        self.count = {}
        self.hwm = {e: {} for e in ENGINES}
        self.sems = {}
        self.msem = {e: "m_" + e for e in ENGINES}

    def sem(self, name):
        if name not in self.sems:
            self.sems[name] = self.nc.alloc_semaphore(name=name)
        return self.sems[name]

    def op(self, eng, fn, waits=(), inc=None, inc_amt=1, waits_hard=()):
        import sys
        f = sys._getframe(1)
        if f.f_code.co_name in ("mop", "dma"):
            f = f.f_back
        self.tags = getattr(self, "tags", {e: [] for e in ENGINES})
        self.tags[eng].append(f.f_lineno)
        if self.serial and fn is not None:
            waits = list(waits) + [(sn, v) for sn, v in self.count.items()]
        ws = []
        own = self.msem[eng]
        for sn, v in waits_hard:
            if v is None or v <= 0:
                continue
            if self.hwm[eng].get(sn, 0) >= v:
                continue
            self.hwm[eng][sn] = v
            ws.append((sn, v))
        for sn, v in waits:
            if v is None or v <= 0:
                continue
            if self.hwm[eng].get(sn, 0) >= v:
                continue
            self.hwm[eng][sn] = v
            ws.append((sn, v))
        ev = None
        if inc is not None:
            self.count[inc] = self.count.get(inc, 0) + inc_amt
            ev = (inc, self.count[inc])
        self.items[eng].append((ws, fn, inc, inc_amt))
        return ev

    def mop(self, eng, fn, waits=(), waits_hard=()):
        return self.op(eng, fn, waits=waits, inc=self.msem[eng], waits_hard=waits_hard)

    def dma(self, queue, out_ap, in_ap, sem_name, waits=()):
        return self.op(queue, lambda e, o=out_ap, i=in_ap: e.dma_start(o, i),
                       waits=waits, inc=sem_name, inc_amt=16)

    def barrier_wait(self, eng, waits):
        self.op(eng, None, waits=waits)

    def emit(self, block):
        for eng in ENGINES:
            items = self.items[eng]

            def body(e, items=items):
                for ws, fn, inc, amt in items:
                    for sn, v in ws:
                        e.wait_ge(self.sem(sn), v)
                    if fn is None:
                        continue
                    ins = fn(e)
                    if inc is not None:
                        ins.then_inc(self.sem(inc), amt)
            getattr(block, _BLOCK_ATTR[eng])(body)

    def stats(self):
        return {e: len(v) for e, v in self.items.items()}


def emax(*evs):
    best = {}
    for ev in evs:
        if ev is None:
            continue
        if isinstance(ev, list):
            for sn, v in ev:
                if v is not None:
                    best[sn] = max(best.get(sn, 0), v)
            continue
        sn, v = ev
        if v is None:
            continue
        best[sn] = max(best.get(sn, 0), v)
    return list(best.items())


def build_program(debug=False, stage="full", serial=False):
    if debug:
        stage = "full"
    nc = bass.Bass("TRN2", target_bir_lowering=False)

    xb_d = nc.dram_tensor("xb", [SEQ, D], F32, kind="ExternalInput")
    xq_d = nc.dram_tensor("xq", [T, D], F32, kind="ExternalInput")
    wq_d = nc.dram_tensor("wq", [H * 128, DT * 128], F32, kind="ExternalInput")
    wk_d = nc.dram_tensor("wk", [H * 128, DT * 128], F32, kind="ExternalInput")
    wv_d = nc.dram_tensor("wv", [NQ * 128, DT * 512], F32, kind="ExternalInput")
    wo_d = nc.dram_tensor("wo", [4 * 128, DT * 512], F32, kind="ExternalInput")
    w1t_d = nc.dram_tensor("w1t", [D, FF], F32, kind="ExternalInput")
    w2t_d = nc.dram_tensor("w2t", [FF, D], F32, kind="ExternalInput")
    b1r_d = nc.dram_tensor("b1r", [128, 64], F32, kind="ExternalInput")
    b2_d = nc.dram_tensor("b2", [1, D], F32, kind="ExternalInput")
    masks_d = nc.dram_tensor("masks", [128, NS * 256], BF16, kind="ExternalInput")
    out_d = nc.dram_tensor("out", [T, D], F32, kind="ExternalOutput")

    dbg = stage != "full"
    full_like = stage in ("full", "F")
    ikind = "ExternalOutput" if dbg else "Internal"
    qT_dram = nc.dram_tensor("qT_dram", [D, T], BF16, kind=ikind)
    zT_dram = nc.dram_tensor("zT_dram", [NS * 128, DT * 128], BF16, kind=ikind)
    scr_dram = nc.dram_tensor("scr_dram", [2, 128], F32, kind="Internal")
    if dbg:
        dbg_lnxT = nc.dram_tensor("dbg_lnxT", [128, DT * SEQ], BF16, kind="ExternalOutput")
        dbg_lnqT = nc.dram_tensor("dbg_lnqT", [128, DT * T], BF16, kind="ExternalOutput")
        dbg_kT = nc.dram_tensor("dbg_kT", [128, HPQ * SEQ], BF16, kind="ExternalOutput")
        dbg_v = nc.dram_tensor("dbg_v", [128, 16 * 512], BF16, kind="ExternalOutput")
        dbg_x1 = nc.dram_tensor("dbg_x1", [T, D], F32, kind="ExternalOutput")
        dbg_ln2T = nc.dram_tensor("dbg_ln2T", [128, DT * T], BF16, kind="ExternalOutput")
        dbg_t = nc.dram_tensor("dbg_t", [128, 2], F32, kind="ExternalOutput")
        dbg_parts = nc.dram_tensor("dbg_parts", [128, 128], F32, kind="ExternalOutput")
        dbg_w1q = nc.dram_tensor("dbg_w1q", [128, DT * FCW], BF16, kind="ExternalOutput")
        dbg_w2q = nc.dram_tensor("dbg_w2q", [128, 4 * D], BF16, kind="ExternalOutput")
        dbg_gT = nc.dram_tensor("dbg_gT", [128, 4 * T], BF16, kind="ExternalOutput")
        dbg_S = nc.dram_tensor("dbg_S", [128, SEQ], F32, kind="ExternalOutput")
        dbg_pat = nc.dram_tensor("dbg_pat", [128, SEQ], BF16, kind="ExternalOutput")
        dbg_patT = nc.dram_tensor("dbg_patT", [128, SEQ], BF16, kind="ExternalOutput")
        dbg_nm = nc.dram_tensor("dbg_nm", [128, 4], F32, kind="ExternalOutput")
        dbg_bn = nc.dram_tensor("dbg_bn", [128, 2], F32, kind="ExternalOutput")
        dbg_lnbf = nc.dram_tensor("dbg_lnbf", [128, D], BF16, kind="ExternalOutput")

    AL = nc.alloc_sbuf_tensor
    base0 = nc.sbuf_base
    id128 = AL("id128", [128, 128], BF16)
    ones_bf = AL("ones_bf", [1, 128], BF16)
    ones_f = AL("ones_f", [1, 128], F32)
    epsc = AL("epsc", [128, 1], F32)
    b1sb = AL("b1sb", [128, 64], F32)
    b2row = AL("b2row", [1, D], BF16)
    t1b = AL("t1b", [128, 1], F32)
    t2b = AL("t2b", [128, 1], F32)
    negt2b = AL("negt2b", [128, 1], F32)
    onescol = AL("onescol", [128, 1], F32)
    part1 = AL("part1", [128, 256], F32)
    part2 = AL("part2", [128, 256], F32)
    scal2 = AL("scal2", [1, 2], F32)
    bn2 = [AL(f"bn2{i}", [128, 2], F32) for i in range(2)]
    ssq4 = [AL(f"ssq4{i}", [128, 4], F32) for i in range(2)]
    sum4 = [AL(f"sum4{i}", [128, 4], F32) for i in range(2)]
    msq = [AL(f"msq{i}", [128, 1], F32) for i in range(2)]
    tmp1 = [AL(f"tmp1{i}", [128, 1], F32) for i in range(2)]
    rstd = [AL(f"rstd{i}", [128, 1], F32) for i in range(2)]
    sume4 = [AL(f"sume4{i}", [128, 4], F32) for i in range(NSB)]
    negt1b = AL("negt1b", [128, 1], F32)
    sume = [AL(f"sume{i}", [128, 1], F32) for i in range(NSB)]
    rs = [AL(f"rs{i}", [128, 1], F32) for i in range(NSB)]
    qov_base = nc.sbuf_base
    masksb = AL("masksb", [128, NS * 256], BF16)
    xstage = [AL(f"xstage{i}", [128, D], F32) for i in range(2)]
    lnbf_off = nc.sbuf_base
    lnbf = AL("lnbf", [128, D], BF16)
    qov_end = nc.sbuf_base

    M0 = nc.sbuf_base

    sub0 = nc.sbuf_base
    lnq2T = AL("lnq2T", [128, DT * T], BF16)
    sub_end = nc.sbuf_base
    nc.sbuf_base = sub0
    Sbuf = [AL(f"S{i}", [128, SEQ], BF16) for i in range(NSB)]
    diagm = [AL(f"diagm{i}", [128, 128], BF16) for i in range(NSB)]
    patT = [AL(f"patT{i}", [128, SEQ], BF16) for i in range(2)]
    zTq = AL("zTq", [128, HPQ * T], BF16)
    nc.sbuf_base = max(nc.sbuf_base, sub_end)

    qstage = [AL(f"qstage{i}", [128, 512], BF16) for i in range(2)]
    qTq_off = nc.sbuf_base
    qTq = AL("qTq", [128, HPQ * T], BF16)
    sv2 = nc.sbuf_base
    nc.sbuf_base = qTq_off
    xstage.append(AL("xstage2", [128, D], F32))
    assert nc.sbuf_base <= sv2
    nc.sbuf_base = sv2
    lnxT = AL("lnxT", [128, DT * SEQ], BF16)
    kTb = AL("kTb", [128, HPQ * SEQ], BF16)
    vqb = AL("vqb", [128, 16 * 512], BF16)
    wkbuf = [AL(f"wkbuf{i}", [128, DT * 128], BF16) for i in range(3)]
    wvq = AL("wvq", [128, DT * 512], BF16)
    att_end = nc.sbuf_base
    assert att_end <= nc.sbuf_top, f"ATT overflow: {att_end} > {nc.sbuf_top}"

    nc.sbuf_base = M0
    x1 = AL("x1", [128, NS * D], F32)
    ln2T = AL("ln2T", [128, DT * T], BF16)
    M1 = nc.sbuf_base
    wobuf = [AL(f"wobuf{i}", [128, DT * 512], BF16) for i in range(2)]
    zTop = [AL(f"zTop{i}", [128, DT * 128], BF16) for i in range(NS)]
    op_end = nc.sbuf_base
    assert op_end <= nc.sbuf_top, "OP overflow"
    nc.sbuf_base = M1
    w1q = [AL(f"w1q{i}", [128, DT * FCW], BF16) for i in range(2)]
    w2q = [AL(f"w2q{i}", [128, 4 * D], BF16) for i in range(2)]
    gTb = AL("gTb", [128, 4 * T], BF16)
    ffn_end = nc.sbuf_base
    assert ffn_end <= nc.sbuf_top, f"FFN overflow {ffn_end} > {nc.sbuf_top}"
    nc.sbuf_base = qov_base
    qst1 = [AL(f"qst1{i}", [128, 1024], F32) for i in range(2)]
    qab1 = [AL(f"qab1{i}", [128, 1024], BF16) for i in range(2)]
    qst2 = [AL(f"qst2{i}", [128, 512], F32) for i in range(2)]
    qsg2 = [AL(f"qsg2{i}", [128, 512], BF16) for i in range(2)]
    assert nc.sbuf_base <= qov_end, f"quant overlay overflow {nc.sbuf_base} > {qov_end}"
    nc.sbuf_base = max(att_end, op_end, ffn_end)

    from contextlib import ExitStack
    stack = ExitStack()
    ps_sc = [stack.enter_context(nc.psum_tensor(f"ps_sc{i}", [128, 512], F32)) for i in range(2)]
    ps_z = [stack.enter_context(nc.psum_tensor(f"ps_z{i}", [128, 512], F32)) for i in range(2)]
    ps_kv = [stack.enter_context(nc.psum_tensor(f"ps_kv{i}", [128, 512], F32)) for i in range(2)]
    psT = [stack.enter_context(nc.psum_tensor(f"psT{i}", [128, 1024], BF16)) for i in range(2)]

    P = Plan(nc, serial=serial)

    def finish(waits):
        P.barrier_wait("sp", waits)
        with nc.Block() as block:
            P.emit(block)
        stack.close()
        return nc, P

    ev_ms = P.mop("pool", lambda e: e.memset(id128[:, :], 0.0))
    ev_id = P.mop("pool", lambda e: e.affine_select(
        out=id128[:, :], in_=id128[:, :], compare_op=ALU.not_equal,
        fill=1.0, base=0, pattern=[[-1, 128]], channel_multiplier=1),
        waits_hard=emax(ev_ms))
    ev_ones = P.mop("pool", lambda e: e.memset(ones_bf[:, :], 1.0))
    ev_onesf = P.mop("pool", lambda e: e.memset(ones_f[:, :], 1.0))
    ev_onescol = P.mop("pool", lambda e: e.memset(onescol[:, :], 1.0))
    ev_const = P.mop("pool", lambda e: e.memset(epsc[:, :], 1e-5))
    ev_b1 = P.dma("sp", b1sb[:, :], b1r_d[:, :], "s_cst")
    ev_msk = P.dma("sp", masksb[:, :], masks_d[:, :], "s_msk")
    ev_b2 = P.dma("pool", b2row[:, :], b2_d[:, :], "s_cstg")

    ev_z_evac_init = [None]
    ev_xs_user = [None, None, None]
    ev_psT_user = [None, None]

    ev_bn2_user = [None, None]
    ev_ssq_user = [None, None]
    ev_sum4_user = [None, None]
    ev_sq_last = None
    junk_ps = [ps_z[0]]

    def ln_stats(src_evs, par, src_ap):
        nonlocal ev_sq_last
        sp_ = par
        evp = None
        for cc in range(4):
            evp = P.mop("dve", lambda e, c_=cc, s_=src_ap: e.reduce_sum(
                sum4[sp_][:, c_:c_ + 1], s_[:, c_ * 512:(c_ + 1) * 512], axis=AX.X),
                waits=emax(src_evs[cc], ev_sum4_user[sp_]))
        ev_rs = P.mop("dve", lambda e: e.reduce_sum(
            rstd[sp_][:, :], sum4[sp_][:, :], axis=AX.X), waits=emax(evp))
        ev_sum4_user[sp_] = ev_rs
        ev_mean = P.mop("act", lambda e: e.activation(
            bn2[sp_][:, 0:1], rstd[sp_][:, :], AF.Copy, scale=1.0 / 2048),
            waits=emax(ev_rs, ev_bn2_user[sp_]))
        evs = None
        for cc in range(4):
            evs = P.mop("act", lambda e, c_=cc, s_=src_ap, j_=junk_ps[0]: e.activation(
                j_[:, :], s_[:, c_ * 512:(c_ + 1) * 512], AF.Square,
                accum_out=ssq4[sp_][:, c_:c_ + 1]),
                waits=emax(src_evs[cc], ev_ssq_user[sp_], ev_z_evac_init[0]),
                waits_hard=emax(ev_sq_last))
            ev_sq_last = evs
        ev_q = P.mop("dve", lambda e: e.reduce_sum(
            tmp1[sp_][:, :], ssq4[sp_][:, :], axis=AX.X), waits=emax(evs))
        ev_ssq_user[sp_] = ev_q
        ev_msq = P.mop("act", lambda e: e.activation(
            msq[sp_][:, :], bn2[sp_][:, 0:1], AF.Square), waits=emax(ev_mean))
        ev_var = P.mop("dve", lambda e: e.tensor_scalar(
            bn2[sp_][:, 1:2], tmp1[sp_][:, :], 1.0 / 2048, msq[sp_][:, :],
            ALU.mult, ALU.subtract), waits=emax(ev_msq), waits_hard=emax(ev_q))
        ev_sqrt = P.mop("act", lambda e: e.activation(
            rstd[sp_][:, :], bn2[sp_][:, 1:2], AF.Sqrt, bias=epsc[:, :]),
            waits=emax(ev_var, ev_const))
        ev_rcp = P.mop("dve", lambda e: e.reciprocal(rstd[sp_][:, :], rstd[sp_][:, :]),
                       waits=emax(ev_sqrt))
        return ev_rcp

    ev_lnbf_half = [None, None]

    def ln_back(xp, sp_, dst, tile_len, dst_col, ev_rcp):
        nonlocal ev_psT_user
        xs = xstage[xp]
        last_ev = None
        ev_ln = None
        for hf in range(2):
            ev_ln = P.mop("dve", lambda e, h_=hf: e.tensor_scalar(
                lnbf[:, h_ * 1024:(h_ + 1) * 1024],
                xs[:, h_ * 1024:(h_ + 1) * 1024],
                bn2[sp_][:, 0:1], rstd[sp_][:, :],
                ALU.subtract, ALU.mult),
                waits=emax(ev_lnbf_half[hf]), waits_hard=emax(ev_rcp))
            tev = None
            for j in range(8):
                dtile = hf * 8 + j
                tev = P.mop("pe", lambda e, d_=dtile, j_=j, b_=hf: e.transpose(
                    psT[b_][:, j_ * 128:(j_ + 1) * 128],
                    lnbf[:, d_ * 128:(d_ + 1) * 128], id128[:, :]),
                    waits=emax(ev_ln, ev_id, ev_psT_user[hf]))
            ev_lnbf_half[hf] = tev
            g0 = hf * 8
            last_ev = P.mop("dve", lambda e, b_=hf, g0_=g0: e.tensor_copy(
                dst[:, :].rearrange("p (t s) -> p t s", s=tile_len)[
                    :, g0_:g0_ + 8, dst_col:dst_col + 128],
                psT[b_][:, :].rearrange("p (t s) -> p t s", s=128)),
                waits=emax(tev))
            ev_psT_user[hf] = last_ev
        ev_bn2_user[sp_] = ev_ln
        ev_xs_user[xp] = ev_ln
        return last_ev

    ev_wkbuf_user = [None, None, None]
    ev_kvps_evac = [None, None]
    qun = 0
    ev_lnq_half = [None, None]

    def qproj_half(h, sh):
        nonlocal qun
        wpar = qun % 3
        ev_wq = P.dma("pool", wkbuf[wpar][:, :], wq_d[h * 128:(h + 1) * 128, :],
                      f"s_wk{wpar}", waits=emax(ev_wkbuf_user[wpar]))
        pp = qun % 2
        mm = None
        for dt_i in range(DT):
            mm = P.mop("pe", lambda e, d_=dt_i, w_=wpar, p_=pp, s_=sh: e.matmul(
                ps_kv[p_][:, :],
                wkbuf[w_][:, d_ * 128:(d_ + 1) * 128],
                lnq2T[:, d_ * T + s_ * 512: d_ * T + s_ * 512 + 512],
                start=(d_ == 0), stop=(d_ == DT - 1)),
                waits=emax(ev_wq, ev_lnq_half[sh], ev_kvps_evac[pp]))
        qp = qun % 2
        ev_qs = P.mop("act", lambda e, p_=pp, q_=qp: e.activation(
            qstage[q_][:, :], ps_kv[p_][:, :], AF.Copy, scale=SCALE),
            waits=emax(mm, (f"s_qo{qp}", 16 * (qun // 2))))
        ev_kvps_evac[pp] = ev_qs
        P.dma("sp", qT_dram[h * 128:(h + 1) * 128, sh * 512: sh * 512 + 512],
              qstage[qp][:, :], f"s_qo{qp}", waits=emax(ev_qs))
        qun += 1
        ev_wkbuf_user[wpar] = mm

    QSCHED = {4: [(0, 0), (1, 0)], 5: [(2, 0), (3, 0)],
              6: [(4, 0), (5, 0)], 7: [(6, 0), (7, 0)],
              8: [(8, 0), (9, 0)], 9: [(10, 0), (11, 0)],
              10: [(12, 0), (13, 0)], 11: [(14, 0), (15, 0)],
              12: [(0, 1), (1, 1)], 13: [(2, 1), (3, 1)],
              14: [(4, 1), (5, 1)], 15: [(6, 1), (7, 1)],
              16: [(8, 1)], 17: [(9, 1)], 18: [(10, 1)], 19: [(11, 1)],
              20: [(12, 1)], 21: [(13, 1)], 22: [(14, 1)], 23: [(15, 1)]}

    tiles = [(xq_d, k * 128, lnq2T, T, k * 128) for k in range(NS)] + \
            [(xb_d, i * 128, lnxT, SEQ, i * 128) for i in range(SEQ // 128)]
    NT = 1 if stage == "A1" else len(tiles)

    def ln_load(t):
        dram, r0, _, _, _ = tiles[t]
        p3 = t % 3
        return P.dma("sp", xstage[p3][:, :], dram[r0:r0 + 128, :],
                     f"s_x{p3}", waits=emax(ev_xs_user[p3]))

    def ln_front(t, ev_load):
        return ln_stats([ev_load] * 4, t % 2, xstage[t % 3][:, :])

    ev_lnq_done = None
    ev_lnx_done = None
    loads = {0: ln_load(0)}
    if NT > 1:
        loads[1] = ln_load(1)
    rcp_evs = {0: ln_front(0, loads.pop(0))}
    for t in range(NT):
        if t + 2 < NT:
            loads[t + 2] = ln_load(t + 2)
        if t + 1 < NT:
            rcp_evs[t + 1] = ln_front(t + 1, loads.pop(t + 1))
        _, _, dst, tl, dc = tiles[t]
        ev = ln_back(t % 3, t % 2, dst, tl, dc, rcp_evs.pop(t))
        if t == 3:
            ev_lnq_half[0] = ev
        if t == NS - 1:
            ev_lnq_done = ev
        ev_lnx_done = ev
        if stage != "A1":
            for (h_, sh_) in QSCHED.get(t, []):
                qproj_half(h_, sh_)
    ev_lnq_half[1] = ev_lnq_done
    if stage == "A1":
        P.dma("sp", dbg_bn[:, :], bn2[0][:, :], "s_dbg", waits=emax(ev_lnx_done))
        P.dma("sp", dbg_lnbf[:, :], lnbf[:, :], "s_dbg", waits=emax(ev_lnx_done))
        ev_d = P.dma("sp", dbg_lnqT[:, :], lnq2T[:, :], "s_dbg", waits=emax(ev_lnx_done))
        return finish(emax(("s_dbg", P.count.get("s_dbg", 0)), ev_d))

    if stage == "A":
        P.dma("sp", dbg_lnqT[:, :], lnq2T[:, :], "s_dbg", waits=emax(ev_lnq_done))
        ev_d = P.dma("sp", dbg_lnxT[:, :], lnxT[:, :], "s_dbg", waits=emax(ev_lnx_done))
        return finish(emax(ev_d, ("s_qo0", 16 * 16), ("s_qo1", 16 * 16)))

    p1_chunks = []
    for c in range(64):
        rt, ch = divmod(c, 4)
        p1_chunks.append((w1t_d, rt * 128, ch * 2048, part1, c))
    for c in range(64):
        p1_chunks.append((w2t_d, c * 128, 0, part2, c))
    p1_state = {"dma": 0, "abs": 0, "red": [None, None],
                "partev": [None, None], "evl": {}}

    def p1_dma():
        i = p1_state["dma"]
        if i >= len(p1_chunks):
            return
        dram, r0, c0, partsb, col = p1_chunks[i]
        par = i % 2
        p1_state["evl"][i] = P.dma(
            "sp", xstage[par][:, :], dram[r0:r0 + 128, c0:c0 + 2048],
            f"s_x{par}", waits=emax(p1_state["red"][par], ev_xs_user[par]))
        ev_xs_user[par] = None
        p1_state["dma"] = i + 1

    def p1_abs():
        j = p1_state["abs"]
        if j >= 4 * min(p1_state["dma"], len(p1_chunks)):
            return
        i, sc = divmod(j, 4)
        dram, r0, c0, partsb, col = p1_chunks[i]
        par = i % 2
        ev_r = P.mop("dve", lambda e, sb_=partsb, c_=col, p_=par, s_=sc:
                     e.tensor_reduce(
            sb_[:, c_ * 4 + s_: c_ * 4 + s_ + 1],
            xstage[p_][:, s_ * 512:(s_ + 1) * 512],
            axis=AX.X, op=ALU.add, apply_absolute_value=True),
            waits=emax(p1_state["evl"][i] if sc == 0 else None))
        if sc == 3:
            p1_state["evl"].pop(i, None)
            p1_state["red"][par] = ev_r
            p1_state["partev"][0 if partsb is part1 else 1] = ev_r
        p1_state["abs"] = j + 1

    def pump_pass1(n):
        for _ in range(n):
            if p1_state["dma"] - p1_state["abs"] // 4 >= 2 or \
               p1_state["dma"] >= len(p1_chunks):
                p1_abs()
            if p1_state["abs"] % 4 == 0 or p1_state["dma"] - p1_state["abs"] // 4 < 2:
                p1_dma()
        while p1_state["dma"] >= len(p1_chunks) and n >= 4 * len(p1_chunks) and \
                p1_state["abs"] < 4 * len(p1_chunks):
            p1_abs()

    ev_qTq_user = None
    ev_zTq_out = None
    ev_zevac_last = None
    ev_wvq_user = None
    ev_masksb_user = None
    ev_kTb_user = None
    ev_kv_k = {}
    ev_kv_v = {}
    ev_S_user = [None] * NSB
    ev_patT_user = [None, None]
    ev_sc_evac = [None, None]
    ev_z_evac = [ev_sq_last, None]
    ev_patTe_last = None
    uidx = 0
    scp = 0

    for g in range(NQ):
        ev_qTq_load = P.dma(
            "sp", qTq[:, :].rearrange("p (h s) -> p h s", s=T),
            qT_dram[g * 512:(g + 1) * 512, :].rearrange("(h e) s -> e h s", e=128),
            "s_qTq",
            waits=emax(ev_qTq_user, ("s_qo0", 16 * 16), ("s_qo1", 16 * 16)))

        ev_wv = P.dma("pool", wvq[:, :], wv_d[g * 128:(g + 1) * 128, :],
                      "s_wv", waits=emax(ev_wvq_user))
        kvp = 0
        for hc in range(HPQ):
            h = g * HPQ + hc
            wpar = hc % 3
            ev_wk = P.dma("pool", wkbuf[wpar][:, :], wk_d[h * 128:(h + 1) * 128, :],
                          f"s_wk{wpar}", waits=emax(ev_wkbuf_user[wpar]))
            mm = None
            for mb in range(4):
                pp = kvp % 2
                kvp += 1
                for dt_i in range(DT):
                    mm = P.mop("pe", lambda e, d_=dt_i, w_=wpar, p_=pp, m_=mb: e.matmul(
                        ps_kv[p_][:, :],
                        wkbuf[w_][:, d_ * 128:(d_ + 1) * 128],
                        lnxT[:, d_ * SEQ + m_ * 512: d_ * SEQ + m_ * 512 + 512],
                        start=(d_ == 0), stop=(d_ == DT - 1)),
                        waits=emax(ev_wk, ev_lnx_done, ev_kvps_evac[pp]))
                ev = P.mop("act", lambda e, p_=pp, hc_=hc, m_=mb: e.activation(
                    kTb[:, hc_ * SEQ + m_ * 512: hc_ * SEQ + m_ * 512 + 512],
                    ps_kv[p_][:, :], AF.Copy), waits=emax(mm))
                ev_kvps_evac[pp] = ev
                pump_pass1(3)
            ev_kv_k[(g, hc)] = ev
            ev_wkbuf_user[wpar] = mm
        for j in range(16):
            pp = kvp % 2
            kvp += 1
            mm = None
            for dt_i in range(DT):
                mm = P.mop("pe", lambda e, d_=dt_i, p_=pp, j_=j: e.matmul(
                    ps_kv[p_][:, :],
                    lnxT[:, d_ * SEQ + j_ * 128: d_ * SEQ + j_ * 128 + 128],
                    wvq[:, d_ * 512:(d_ + 1) * 512],
                    start=(d_ == 0), stop=(d_ == DT - 1)),
                    waits=emax(ev_wv, ev_lnx_done, ev_kvps_evac[pp]))
            ev = P.mop("act", lambda e, p_=pp, j_=j: e.activation(
                vqb[:, j_ * 512:(j_ + 1) * 512], ps_kv[p_][:, :], AF.Copy),
                waits=emax(mm))
            ev_kvps_evac[pp] = ev
            ev_kv_v[(g, j)] = ev
            pump_pass1(3)
        ev_wvq_user = mm

        def unit_front(k, hc):
            nonlocal uidx, scp, ev_qTq_user, ev_masksb_user, ev_kTb_user
            ext = EXT[k]
            W = ext * 128
            nch = (ext + 3) // 4
            par = uidx % NSB
            u_g = uidx
            uidx += 1
            ev_e = None
            for c in range(nch):
                wc = min(512, W - c * 512)
                pp = scp % 2
                scp += 1
                last = c == nch - 1
                mm = P.mop("pe", lambda e, p_=pp, c_=c, w_=wc, k_=k, h_=hc, l_=last:
                           e.matmul(
                    ps_sc[p_][:, 0:w_],
                    qTq[:, h_ * T + k_ * 128: h_ * T + k_ * 128 + 128],
                    kTb[:, h_ * SEQ + c_ * 512: h_ * SEQ + c_ * 512 + w_],
                    start=True, stop=not l_),
                    waits=emax(ev_qTq_load, ev_kv_k[(g, hc)], ev_sc_evac[pp]))
                ev_qTq_user = mm
                ev_kTb_user = mm
                if last:
                    moff = wc - 256
                    mm = P.mop("pe", lambda e, p_=pp, mo_=moff, k_=k: e.matmul(
                        ps_sc[p_][:, mo_:mo_ + 256], id128[:, :],
                        masksb[:, k_ * 256:(k_ + 1) * 256],
                        start=False, stop=True),
                        waits=emax(ev_msk, ev_id))
                    ev_masksb_user = mm
                c0 = c * 512
                ev_e = P.mop("act", lambda e, p_=pp, q_=par, c0_=c0, w_=wc, c_=c:
                             e.activation(Sbuf[q_][:, c0_:c0_ + w_],
                                          ps_sc[p_][:, 0:w_], AF.Exp,
                                          accum_out=sume4[q_][:, c_:c_ + 1]),
                             waits=emax(mm, ev_S_user[par]))
                ev_sc_evac[pp] = ev_e
            ev_red = P.mop("dve", lambda e, q_=par, n_=nch: e.reduce_sum(
                sume[q_][:, :], sume4[q_][:, 0:n_], axis=AX.X),
                waits=emax(ev_e))
            ev_rcp = P.mop("dve", lambda e, q_=par: e.reciprocal(rs[q_][:, :], sume[q_][:, :]),
                           waits=emax(ev_red))
            ev_dg = P.mop("dve", lambda e, q_=par, w_=W: e.tensor_scalar(
                Sbuf[q_][:, 0:w_], Sbuf[q_][:, 0:w_], rs[q_][:, :], None, ALU.mult),
                waits=emax(ev_id), waits_hard=emax(ev_rcp))
            return {"k": k, "hc": hc, "par": par, "ext": ext, "ev_dg": ev_dg,
                    "u": u_g}

        def unit_back(st):
            nonlocal ev_psT_user, ev_patT_user, ev_patTe_last
            nonlocal ev_zevac_last
            k, hc, par, ext = st["k"], st["hc"], st["par"], st["ext"]
            ev_dg = st["ev_dg"]
            zpp = st["u"] % 2
            up = st["u"] % 2
            ngrp = (ext + 7) // 8
            grp_ev = []
            tev = None
            for grp in range(ngrp):
                bank = grp % 2
                n_in = min(8, ext - grp * 8)
                for j in range(n_in):
                    mt = grp * 8 + j
                    tev = P.mop("pe", lambda e, b_=bank, j_=j, m_=mt, q_=par: e.transpose(
                        psT[b_][:, j_ * 128:(j_ + 1) * 128],
                        Sbuf[q_][:, m_ * 128:(m_ + 1) * 128], id128[:, :]),
                        waits=emax(ev_dg, ev_psT_user[bank]))
                pev = P.mop("dve", lambda e, b_=bank, g_=grp, n_=n_in, u_=up: e.tensor_copy(
                    patT[u_][:, g_ * 1024: g_ * 1024 + n_ * 128],
                    psT[b_][:, 0:n_ * 128]),
                    waits=emax(tev, ev_patT_user[up]))
                ev_psT_user[bank] = pev
                ev_patTe_last = pev
                grp_ev.append(pev)
            ev_S_user[par] = tev
            mm = None
            for mt in range(ext):
                mm = P.mop("pe", lambda e, p_=zpp, m_=mt, h_=hc, x_=ext, u_=up: e.matmul(
                    ps_z[p_][:, 0:128],
                    vqb[:, m_ * 512 + h_ * 128: m_ * 512 + h_ * 128 + 128],
                    patT[u_][:, m_ * 128:(m_ + 1) * 128],
                    start=(m_ == 0), stop=(m_ == x_ - 1)),
                    waits=emax(grp_ev[mt // 8], ev_kv_v[(g, mt)], ev_z_evac[zpp]))
            ev_patT_user[up] = mm
            ev_zevac_last = P.mop("act", lambda e, p_=zpp, h_=hc, k_=k: e.activation(
                zTq[:, h_ * T + k_ * 128: h_ * T + k_ * 128 + 128],
                ps_z[p_][:, 0:128], AF.Copy),
                waits=emax(mm, ("s_zo", 16 * NS * g)))
            ev_z_evac[zpp] = ev_zevac_last

        def flush_back(st):
            nonlocal ev_zTq_out
            unit_back(st)
            if st["hc"] == HPQ - 1:
                k = st["k"]
                ev_zTq_out = P.dma(
                    "sp",
                    zT_dram[k * 128:(k + 1) * 128,
                            g * 512:(g + 1) * 512].rearrange("e (h s) -> e h s", s=128),
                    zTq[:, :].rearrange("p (h q) -> p h q", q=T)[:, :, k * 128:(k + 1) * 128],
                    "s_zo", waits=emax(ev_zevac_last))

        pend = []
        for k in range(NS):
            for hc in range(HPQ):
                st = unit_front(k, hc)
                pump_pass1(1)
                pend.append(st)
                if len(pend) >= 5:
                    flush_back(pend.pop(0))
        while pend:
            flush_back(pend.pop(0))
    pump_pass1(4 * len(p1_chunks))

    if stage == "B":
        P.dma("sp", dbg_S[:, :], Sbuf[1][:, :], "s_dbg", waits=emax(ev_S_user[1], ev_patT_user))
        P.dma("sp", dbg_pat[:, :], pat[1][:, :], "s_dbg", waits=emax(ev_patT_user))
        P.dma("sp", dbg_patT[:, :], patT[:, :], "s_dbg", waits=emax(ev_patT_user))
        evsm = P.mop("dve", lambda e: e.tensor_copy(ssq4[0][:, 0:1], sume4[1][:, 0:1]),
                     waits=emax(ev_patT_user))
        evsm = P.mop("dve", lambda e: e.tensor_copy(ssq4[0][:, 1:2], sume[1][:, :]))
        evsm = P.mop("dve", lambda e: e.tensor_copy(ssq4[0][:, 2:3], rs[1][:, :]))
        P.dma("sp", dbg_nm[:, :], ssq4[0][:, :], "s_dbg", waits=emax(evsm))
        P.dma("sp", dbg_kT[:, :], kTb[:, :], "s_dbg", waits=emax(ev_kv_k[(3, 3)]))
        P.dma("sp", dbg_v[:, :], vqb[:, :], "s_dbg", waits=emax(ev_kv_v[(3, 15)]))
        ev_pc = P.mop("dve", lambda e: e.tensor_copy(
            xstage[0][:, 0:64], part1[:, :]),
            waits=emax(p1_state["red"][0], p1_state["red"][1]))
        ev_pc2 = P.mop("dve", lambda e: e.tensor_copy(xstage[0][:, 64:128], part2[:, :]))
        ev_pd = P.dma("sp", dbg_parts[:, :], xstage[0][:, 0:128], "s_dbg",
                      waits=emax(ev_pc2))
        return finish(emax(ev_pd, ("s_zo", 16 * NS * NQ), ("s_dbg", P.count.get("s_dbg", 0))))

    ev_att_done = emax(ev_patT_user[0], ev_patT_user[1], ev_qTq_user,
                       ev_zevac_last, ev_patTe_last,
                       ("s_zo", 16 * NS * NQ), ("s_qo0", 16 * 16), ("s_qo1", 16 * 16),
                       *ev_S_user)

    mmp1 = P.mop("pe", lambda e: e.matmul(
        ps_kv[0][0:1, 0:256], onescol[:, :], part1[:, :], start=True, stop=True),
        waits=emax(ev_onescol, p1_state["partev"][0], ev_kvps_evac[0]))
    mmp2 = P.mop("pe", lambda e: e.matmul(
        ps_kv[1][0:1, 0:256], onescol[:, :], part2[:, :], start=True, stop=True),
        waits=emax(p1_state["partev"][1], ev_kvps_evac[1]))
    ev_s1 = P.mop("dve", lambda e: e.tensor_reduce(
        scal2[0:1, 0:1], ps_kv[0][0:1, 0:256], axis=AX.X, op=ALU.add),
        waits=emax(mmp1))
    ev_s2 = P.mop("dve", lambda e: e.tensor_reduce(
        scal2[0:1, 1:2], ps_kv[1][0:1, 0:256], axis=AX.X, op=ALU.add),
        waits=emax(mmp2))
    CQ = 0.7 / float(FF * D)
    ev_t1s = P.mop("act", lambda e: e.activation(
        scal2[0:1, 0:1], scal2[0:1, 0:1], AF.Copy, scale=CQ), waits=emax(ev_s1))
    ev_t2s = P.mop("act", lambda e: e.activation(
        scal2[0:1, 1:2], scal2[0:1, 1:2], AF.Copy, scale=CQ), waits=emax(ev_s2))

    def thresh_back():
        mm1 = P.mop("pe", lambda e: e.matmul(ps_kv[0][:, 0:1], ones_f[:, :],
                                             scal2[0:1, 0:1], start=True, stop=True),
                    waits=emax(ev_t1s, ev_onesf, ev_kvps_evac[0]))
        mm2 = P.mop("pe", lambda e: e.matmul(ps_kv[1][:, 0:1], ones_f[:, :],
                                             scal2[0:1, 1:2], start=True, stop=True),
                    waits=emax(ev_t2s, ev_kvps_evac[1]))
        e1 = P.mop("dve", lambda e: e.tensor_copy(t1b[:, :], ps_kv[0][:, 0:1]),
                   waits=emax(mm1))
        e2 = P.mop("dve", lambda e: e.tensor_copy(t2b[:, :], ps_kv[1][:, 0:1]),
                   waits=emax(mm2))
        e3 = P.mop("dve", lambda e: e.tensor_scalar(
            negt1b[:, :], t1b[:, :], -1.0, None, ALU.mult), waits=emax(e1))
        e4 = P.mop("dve", lambda e: e.tensor_scalar(
            negt2b[:, :], t2b[:, :], -1.0, None, ALU.mult), waits=emax(e2))
        return e1, e2, e3, e4

    ev_qst1_user = [emax(ev_masksb_user), emax(p1_state["red"][0])]
    ev_qab1_user = [emax(p1_state["red"][0]), emax(p1_state["red"][0])]
    ev_qst2_user = [emax(p1_state["red"][1]), emax(p1_state["red"][1])]
    ev_qsg2_user = [emax(p1_state["red"][1]), emax(p1_state["red"][1])]
    ev_w1q_ready = [None, None]
    ev_w2q_ready = [None, None]
    ev_w1p = {}
    ev_w2p = {}
    ev_w1q_user = [None, None]
    ev_w2q_user = [None, None]

    def quant_w1_piece(c, p):
        par = p % 2
        src_ap = w1t_d[p * 256:(p + 1) * 256,
                       c * FCW:(c + 1) * FCW].rearrange("(t q) f -> q t f", q=128)
        ev_l = P.dma("sp", qst1[par][:, :].rearrange("q (t f) -> q t f", f=512),
                     src_ap, f"s_w1{par}", waits=emax(ev_qst1_user[par]))
        ev_b = P.mop("dve", lambda e, p_=par: e.tensor_scalar(
            qab1[p_][:, :], qst1[p_][:, :], negt1b[:, :], None, ALU.is_le),
            waits=emax(ev_l, ev_qab1_user[par]), waits_hard=emax(ev_negt))
        dcol = 2 * p * FCW
        last = P.mop("dve", lambda e, p_=par, dc_=dcol, c_=c: e.scalar_tensor_tensor(
            w1q[c_ % 2][:, dc_:dc_ + 1024], qst1[p_][:, :], t1b[:, :],
            qab1[p_][:, :], ALU.is_ge, ALU.subtract),
            waits=emax(ev_w1q_user[c % 2]), waits_hard=emax(ev_b, ev_t1b))
        ev_qst1_user[par] = last
        ev_qab1_user[par] = last
        ev_w1p[(c, p)] = last
        if p == 7:
            ev_w1q_ready[c % 2] = last

    def quant_w2_piece(c, p, par=None):
        par = (p % 2) if par is None else par
        rt, hf = divmod(p, 4)
        src_ap = w2t_d[c * FCW + rt * 128: c * FCW + (rt + 1) * 128,
                       hf * 512:(hf + 1) * 512]
        ev_l = P.dma("pool", qst2[par][:, :], src_ap,
                     f"s_w2{par}", waits=emax(ev_qst2_user[par]))
        ev_sg = P.mop("dve", lambda e, p_=par: e.tensor_scalar(
            qsg2[p_][:, :], qst2[p_][:, :], negt2b[:, :], None, ALU.is_le),
            waits=emax(ev_l, ev_qsg2_user[par]), waits_hard=emax(ev_negt2))
        dcol = rt * D + hf * 512
        last = P.mop("dve", lambda e, p_=par, dc_=dcol, c_=c: e.scalar_tensor_tensor(
            w2q[c_ % 2][:, dc_:dc_ + 512], qst2[p_][:, :], t2b[:, :],
            qsg2[p_][:, :], ALU.is_ge, ALU.subtract),
            waits=emax(ev_w2q_user[c % 2]), waits_hard=emax(ev_sg, ev_t2b))
        ev_qst2_user[par] = last
        ev_qsg2_user[par] = last
        ev_w2p[(c, p)] = last
        if p == 15:
            ev_w2q_ready[c % 2] = last

    ev_x1_load = P.dma("act", x1[:, :].rearrange("p (s d) -> p s d", d=D),
                       xq_d[:, :].rearrange("(s p) d -> p s d", p=128),
                       "s_x1in", waits=ev_att_done)
    ev_zl = {}
    for k in range(NS):
        ev_zl[k] = P.dma("sp", zTop[k][:, :], zT_dram[k * 128:(k + 1) * 128, :],
                         "s_zi0", waits=[("s_zo", 16 * NS * NQ)])
    opsum = [ps_sc[0], ps_sc[1], ps_z[0], ps_z[1]]
    opev = [ev_sc_evac[0], ev_sc_evac[1], ev_z_evac[0], ev_z_evac[1]]
    ev_wo_user = [None, None]
    ev_x1_evac = {}
    opp = 0
    mm = None
    ev_wo_dm2 = None
    for dm in range(4):
        wpar = dm % 2
        if dm == 0:
            wo_w = emax(ev_wvq_user)
        elif dm == 1:
            wo_w = emax(ev_kTb_user)
        else:
            wo_w = emax(ev_wo_user[wpar])
        ev_wo = P.dma("pool", wobuf[wpar][:, :], wo_d[dm * 128:(dm + 1) * 128, :],
                      f"s_wo{wpar}", waits=wo_w)
        if dm == 3:
            ev_w1q_user[0] = ev_wo_dm2
        for k in range(NS):
            pp = opp % 4
            opp += 1
            for ht in range(DT):
                mm = P.mop("pe", lambda e, p_=pp, t_=ht, k_=k, w_=wpar: e.matmul(
                    opsum[p_][:, :],
                    zTop[k_][:, t_ * 128:(t_ + 1) * 128],
                    wobuf[w_][:, t_ * 512:(t_ + 1) * 512],
                    start=(t_ == 0), stop=(t_ == DT - 1)),
                    waits=emax(ev_zl[k], ev_wo, opev[pp]))
            ev = P.mop("dve", lambda e, p_=pp, k_=k, d_=dm: e.tensor_tensor(
                x1[:, k_ * D + d_ * 512: k_ * D + d_ * 512 + 512],
                x1[:, k_ * D + d_ * 512: k_ * D + d_ * 512 + 512],
                opsum[p_][:, :], ALU.add),
                waits=emax(mm, ev_x1_load))
            opev[pp] = ev
            ev_x1_evac[(dm, k)] = ev
            if dm == 3:
                quant_w1_piece(0, k)
        ev_wo_user[wpar] = mm
        if dm == 2:
            ev_wo_dm2 = mm
            ev_t1b, ev_t2b, ev_negt, ev_negt2 = thresh_back()
    ev_oproj_pe_done = mm
    ev_sc_evac[0] = opev[0]
    ev_sc_evac[1] = opev[1]
    ev_w1q_user[1] = ev_oproj_pe_done
    ev_w2q_user[0] = ev_oproj_pe_done
    ev_w2q_user[1] = ev_oproj_pe_done

    junk_ps[0] = ps_kv[0]
    ev_z_evac_init[0] = ev_t1b
    ev_ln2_done = None

    def ln2_back(k, ev_rcp):
        nonlocal ev_ln2_done
        sp_ = k % 2
        for hf in range(2):
            ev_ln = P.mop("dve", lambda e, k_=k, s_=sp_, h_=hf: e.tensor_scalar(
                lnbf[:, h_ * 1024:(h_ + 1) * 1024],
                x1[:, k_ * D + h_ * 1024: k_ * D + h_ * 1024 + 1024],
                bn2[s_][:, 0:1], rstd[s_][:, :],
                ALU.subtract, ALU.mult),
                waits=emax(ev_lnbf_half[hf]), waits_hard=emax(ev_rcp))
            tev = None
            for j in range(8):
                dtile = hf * 8 + j
                tev = P.mop("pe", lambda e, d_=dtile, j_=j, b_=hf: e.transpose(
                    psT[b_][:, j_ * 128:(j_ + 1) * 128],
                    lnbf[:, d_ * 128:(d_ + 1) * 128], id128[:, :]),
                    waits=emax(ev_ln, ev_psT_user[hf]))
            ev_lnbf_half[hf] = tev
            g0 = hf * 8
            ev_ln2_done = P.mop("dve", lambda e, b_=hf, g0_=g0, k_=k: e.tensor_copy(
                ln2T[:, :].rearrange("p (t s) -> p t s", s=T)[
                    :, g0_:g0_ + 8, k_ * 128:(k_ + 1) * 128],
                psT[b_][:, :].rearrange("p (t s) -> p t s", s=128)),
                waits=emax(tev))
            ev_psT_user[hf] = ev_ln2_done
            ev_bn2_user[sp_] = ev_ln

    def ln2_front(k):
        return ln_stats([ev_x1_evac[(c, k)] for c in range(4)], k % 2,
                        x1[:, k * D:(k + 1) * D])

    W2ORD = [dm_ + 4 * ft_ for dm_ in range(4) for ft_ in range(4)]
    ln2_rcp = {0: ln2_front(0)}
    for k in range(NS):
        if k + 1 < NS:
            ln2_rcp[k + 1] = ln2_front(k + 1)
        ln2_back(k, ln2_rcp.pop(k))
        quant_w2_piece(0, W2ORD[2 * k], par=0)
        quant_w2_piece(0, W2ORD[2 * k + 1], par=1)

    if stage == "C":
        P.mop("dve", lambda e: e.tensor_copy(bn2[0][:, 0:1], t1b[:, :]), waits=emax(ev_t1b))
        ev_dbt = P.mop("dve", lambda e: e.tensor_copy(bn2[0][:, 1:2], t2b[:, :]), waits=emax(ev_t2b))
        P.dma("sp", dbg_t[:, :], bn2[0][:, :], "s_dbg", waits=emax(ev_dbt))
        P.dma("sp", dbg_x1[:, :].rearrange("(s p) d -> p s d", p=128),
              x1[:, :].rearrange("p (s d) -> p s d", d=D),
              "s_dbg", waits=emax(*[ev_x1_evac[(dm, k)] for dm in range(4) for k in range(NS)]))
        ev_d2 = P.dma("sp", dbg_ln2T[:, :], ln2T[:, :], "s_dbg", waits=emax(ev_ln2_done))
        return finish([("s_dbg", P.count.get("s_dbg", 0))])

    ev_gelu = {}
    psF2 = [ps_z[0], ps_z[1], ps_kv[0], ps_kv[1]]
    ev_f2 = [opev[2], opev[3], ev_sq_last, ev_t2b]
    ffp1 = 0
    ffp2 = 0
    ev_ffn_out = []
    for c in range(NFC):
        for fs in range(4):
            fi = c * 4 + fs
            for tb_i in range(2):
                pp = ffp1 % 2
                ffp1 += 1
                mm = None
                for dt_i in range(DT):
                    mm = P.mop("pe", lambda e, d_=dt_i, p_=pp, c_=c, f_=fs, t_=tb_i: e.matmul(
                        ps_sc[p_][:, :],
                        w1q[c_ % 2][:, d_ * FCW + f_ * 128: d_ * FCW + f_ * 128 + 128],
                        ln2T[:, d_ * T + t_ * 512: d_ * T + t_ * 512 + 512],
                        start=(d_ == 0), stop=(d_ == DT - 1)),
                        waits=emax(ev_w1p[(c, dt_i // 2)], ev_ln2_done, ev_sc_evac[pp]))
                ev_g = P.mop("act", lambda e, p_=pp, f_=fs, t_=tb_i, fi_=fi: e.activation(
                    gTb[:, f_ * T + t_ * 512: f_ * T + t_ * 512 + 512],
                    ps_sc[p_][:, :], AF.Gelu, bias=b1sb[:, fi_:fi_ + 1]),
                    waits=emax(mm, ev_b1))
                ev_sc_evac[pp] = ev_g
                ev_gelu[(c, fs, tb_i)] = ev_g
            if c + 1 < NFC:
                quant_w1_piece(c + 1, 2 * fs)
                quant_w1_piece(c + 1, 2 * fs + 1)
        ev_w1q_user[c % 2] = mm
        for tc in range(NS):
            for dm in range(4):
                pp = ffp2 % 4
                ffp2 += 1
                mm = None
                for ft in range(4):
                    mm = P.mop("pe", lambda e, p_=pp, f_=ft, t_=tc, d_=dm, c_=c: e.matmul(
                        psF2[p_][:, :],
                        gTb[:, f_ * T + t_ * 128: f_ * T + t_ * 128 + 128],
                        w2q[c_ % 2][:, f_ * D + d_ * 512: f_ * D + d_ * 512 + 512],
                        start=(f_ == 0), stop=(f_ == 3 and c_ != 0)),
                        waits=emax(ev_gelu[(c, ft, tc // 4)],
                                   ev_w2p[(c, ft * 4 + dm)], ev_f2[pp]))
                if c == 0:
                    mm = P.mop("pe", lambda e, p_=pp, d_=dm: e.matmul(
                        psF2[p_][:, :], ones_bf[:, :],
                        b2row[0:1, d_ * 512:(d_ + 1) * 512],
                        start=False, stop=True),
                        waits=emax(ev_b2, ev_ones))
                ev = P.mop("dve", lambda e, t_=tc, d_=dm, p_=pp: e.tensor_tensor(
                    x1[:, t_ * D + d_ * 512: t_ * D + d_ * 512 + 512],
                    x1[:, t_ * D + d_ * 512: t_ * D + d_ * 512 + 512],
                    psF2[p_][:, :], ALU.add),
                    waits=emax(mm))
                ev_f2[pp] = ev
                if c == NFC - 1:
                    ev_ffn_out.append(ev)
            if c + 1 < NFC:
                quant_w2_piece(c + 1, 2 * tc)
                quant_w2_piece(c + 1, 2 * tc + 1)
        ev_w2q_user[c % 2] = mm

    P.dma("sp", out_d[:, :].rearrange("(s p) d -> p s d", p=128),
          x1[:, :].rearrange("p (s d) -> p s d", d=D),
          "s_out", waits=emax(*ev_ffn_out))
    if stage == "F":
        P.dma("sp", dbg_w1q[:, :], w1q[(NFC - 1) % 2][:, :], "s_dbg",
              waits=emax(*ev_ffn_out))
        P.dma("sp", dbg_w2q[:, :], w2q[(NFC - 1) % 2][:, :], "s_dbg")
        P.dma("sp", dbg_gT[:, :], gTb[:, :], "s_dbg")
        P.mop("dve", lambda e: e.tensor_copy(bn2[0][:, 0:1], t1b[:, :]))
        ev_dbt = P.mop("dve", lambda e: e.tensor_copy(bn2[0][:, 1:2], t2b[:, :]))
        P.dma("sp", dbg_t[:, :], bn2[0][:, :], "s_dbg", waits=emax(ev_dbt))
        return finish([("s_out", 16), ("s_dbg", P.count.get("s_dbg", 0))])
    return finish([("s_out", 16)])


def _hm(wT, nb, eb):
    return np.ascontiguousarray(
        wT.reshape(DT, 128, nb, eb).transpose(2, 1, 0, 3).reshape(nb * 128, DT * eb))


def _build_masks(blocks):
    ets = [b + 1 for b in blocks]
    m = np.zeros((128, NS * 256), np.float32)
    tri = np.triu(np.ones((128, 128), dtype=bool), 1)
    for k in range(NS):
        Ek, et = EXT[k], ets[k]
        assert et <= Ek
        for t in range(2):
            jj = Ek - 2 + t
            w = m[:, k * 256 + t * 128: k * 256 + (t + 1) * 128]
            if jj == et - 1:
                w[tri] = -1e9
            elif jj > et - 1:
                w[:, :] = -1e9
    return m


_CACHE = {}


def kernel(**inputs):
    x = np.ascontiguousarray(np.asarray(inputs["L0_attn_x"], dtype=np.float32))
    WQ = np.asarray(inputs["L0_attn_W_Q"], dtype=np.float32)
    WK = np.asarray(inputs["L0_attn_W_K"], dtype=np.float32)
    WV = np.asarray(inputs["L0_attn_W_V"], dtype=np.float32)
    WO = np.asarray(inputs["L0_attn_W_O"], dtype=np.float32)
    w1 = np.asarray(inputs["L0_ffn_w1"], dtype=np.float32)
    b1 = np.asarray(inputs["L0_ffn_b1"], dtype=np.float32)
    w2 = np.asarray(inputs["L0_ffn_w2"], dtype=np.float32)
    b2 = np.asarray(inputs["L0_ffn_b2"], dtype=np.float32)

    wqT = _hm(WQ.transpose(2, 0, 1).reshape(D, D), H, 128)
    wkT = _hm(WK.transpose(2, 0, 1).reshape(D, D), H, 128)
    wvT = _hm(WV.transpose(2, 0, 1).reshape(D, D), NQ, 512)
    woT = _hm(WO.transpose(0, 2, 1).reshape(D, D), 4, 512)
    w1t = np.ascontiguousarray(w1.T)
    w2t = np.ascontiguousarray(w2.T)
    b1r = np.ascontiguousarray(b1.reshape(64, 128).T)
    b2r = np.ascontiguousarray(b2.reshape(1, D))
    masks_e = _build_masks(BLOCKS_EVEN).astype(BF)
    masks_o = _build_masks(BLOCKS_ODD).astype(BF)

    if "prog" not in _CACHE:
        _CACHE["prog"] = build_program(debug=False)
    nc, _ = _CACHE["prog"]

    in_maps = []
    for core in range(8):
        b = core // 2
        blocks = BLOCKS_EVEN if core % 2 == 0 else BLOCKS_ODD
        xb = np.ascontiguousarray(x[b])
        xq = np.ascontiguousarray(
            np.concatenate([x[b][blk * 128:(blk + 1) * 128] for blk in blocks], axis=0))
        in_maps.append({
            "xb": xb, "xq": xq, "wq": wqT, "wk": wkT, "wv": wvT, "wo": woT,
            "w1t": w1t, "w2t": w2t, "b1r": b1r, "b2": b2r,
            "masks": masks_e if core % 2 == 0 else masks_o,
        })

    res = run_bass_kernel_spmd(nc, in_maps, core_ids=list(range(8)))

    out = np.empty((4, SEQ, D), dtype=np.float32)
    for core in range(8):
        b = core // 2
        blocks = BLOCKS_EVEN if core % 2 == 0 else BLOCKS_ODD
        loc = np.asarray(res.results[core]["out"])
        for k, blk in enumerate(blocks):
            out[b, blk * 128:(blk + 1) * 128] = loc[k * 128:(k + 1) * 128]
    return out
